# version 01
# baseline (speedup 1.0000x reference)
import numpy as np
import ml_dtypes
from contextlib import ExitStack

import concourse.bass as bass
import concourse.mybir as mybir
from concourse.bass_utils import run_bass_kernel_spmd

F32 = mybir.dt.float32
BF16 = mybir.dt.bfloat16
I32 = mybir.dt.int32
AF = mybir.ActivationFunctionType
ALU = mybir.AluOpType

H, S, FF, II, G = 4096, 1024, 9984, 11008, 6
HC, SC, FC, IC = H // 128, S // 128, FF // 128, II // 128
NCORES, TLOC = 8, 1024
THD, EPS = 2.5, 1e-5

PERM = np.array([8 * (p % 16) + p // 16 for p in range(128)])


def _dev_order(n):
    j = np.arange(n)
    return 128 * (j // 128) + PERM[j % 128]


DEV_H = _dev_order(H)

ENGINES = ("sync", "gpsimd", "vector", "scalar", "tensor")
TAPS = False
DBG = False
DEBUG_MEM = False


class Sem:
    def __init__(self, stack, nc, name):
        self.h = stack.enter_context(nc.semaphore(name))
        self.v = 0

    def inc(self, n):
        self.v += n
        return self.v


class B:

    def __init__(self):
        self.th = {e: [] for e in ENGINES}
        self.last_wait = {e: {} for e in ENGINES}

    def E(self, eng, fn):
        self.th[eng].append(fn)

    def wait(self, eng, sem, val):
        if val is None or val <= 0:
            return
        k = id(sem)
        if self.last_wait[eng].get(k, -1) >= val:
            return
        self.last_wait[eng][k] = val
        self.E(eng, lambda e, s=sem.h, v=val: e.wait_ge(s, v))

    def waitp(self, eng, pair):
        if pair is not None:
            self.wait(eng, pair[0], pair[1])

    def dma(self, eng, sem, out, in_):
        tgt = sem.inc(16)
        self.E(eng, lambda e, o=out, i=in_, s=sem.h: e.dma_start(out=o, in_=i).then_inc(s, 16))
        return tgt

    def mm(self, out, lhsT, rhs, start, stop, sem=None):
        tgt = sem.inc(1) if sem else None
        sh = sem.h if sem else None

        def f(e, o=out, l=lhsT, r=rhs, st=start, sp=stop, s=sh):
            i = e.matmul(o, l, r, start=st, stop=sp)
            if s is not None:
                i.then_inc(s, 1)
        self.E("tensor", f)
        return tgt

    def op(self, eng, sem, meth, *args, **kw):
        tgt = sem.inc(1) if sem else None
        sh = sem.h if sem else None

        def f(e, m=meth, a=args, k=kw, s=sh):
            i = getattr(e, m)(*a, **k)
            if s is not None:
                i.then_inc(s, 1)
        self.E(eng, f)
        return tgt


def build_program():
    nc = bass.Bass("TRN2", target_bir_lowering=False)
    b = B()
    stack = ExitStack()
    DT = {}

    def din(name, shape, dtype):
        DT[name] = nc.dram_tensor(name, list(shape), dtype, kind="ExternalInput").ap()
        return DT[name]

    def dscr(name, shape, dtype):
        kind = "ExternalOutput" if TAPS else "Internal"
        DT[name] = nc.dram_tensor(name, list(shape), dtype, kind=kind).ap()
        return DT[name]

    XT = din("XT", (128, HC, TLOC), F32)
    W_IN = din("W_IN", (16, 128, HC * 128), F32)
    W_OUT1 = din("W_OUT1", (HC, 128, 56 * 128), F32)
    W_AGU = din("W_AGU", (2 * FC, 128, HC * 128), F32)
    W_AD = din("W_AD", (HC, 128, FC * 128), F32)
    QGU = din("QGU", (HC, 2, 128, II), I32)
    SGU = din("SGU", (2, 32, II), F32)
    QZGU = din("QZGU", (2, 32, II // 8), I32)
    QD = din("QD", (IC, 128, H), I32)
    SD = din("SD", (86, H), F32)
    QZD = din("QZD", (86, H // 8), I32)
    LPD = din("LPD", (128, 8), F32)
    DFD = din("DFD", (1, 8), F32)
    SEL32N = din("SEL32N", (128, 32 * 32), BF16)
    SEL86N = din("SEL86N", (86, 128, 86), BF16)
    BC32 = din("BC32", (32, 32, 128), BF16)
    BC86 = din("BC86", (86, 86, 128), BF16)
    ONESC = din("ONESC", (128, 1), BF16)
    ONESF = din("ONESF", (128, 1), F32)
    ONESR = din("ONESR", (1, 128), F32)
    SHIFTS = din("SHIFTS", (128, 1), I32)

    OUT = nc.dram_tensor("OUT", [128, HC, TLOC], F32, kind="ExternalOutput").ap()

    XN_SP = dscr("XN_SP", (128, HC * TLOC), BF16)
    AUT_SP = dscr("AUT_SP", (128, 56 * TLOC), BF16)
    HA_SP = dscr("HA_SP", (128, FC * TLOC), BF16)
    HG_SP = dscr("HG_SP", (128, 2, IC, 512), BF16)
    PA = dscr("PA", (128, HC, TLOC), F32)
    PB = dscr("PB", (128, HC, TLOC), F32)
    WDQ_GU = dscr("WDQ_GU", (HC, 2, 128, II), BF16)
    WDQ_D = dscr("WDQ_D", (IC, 128, H), BF16)
    SZP_GU = dscr("SZP_GU", (32, 2 * II), BF16)
    SZP_D = dscr("SZP_D", (86, H), BF16)

    sem_names = ["mm", "ev", "sc", "gp",
                 "ld_c", "ld_x", "ld_win", "ld_w1", "ld_agu", "ld_ad",
                 "ld_q", "ld_s", "ld_z", "ld_bc", "ld_sel",
                 "ld_xn", "ld_aut", "ld_ha", "ld_hg", "ld_w5", "ld_w6",
                 "ld_pax", "sp_xn", "sp_aut", "sp_ha", "sp_hg",
                 "sp_pa", "sp_pb", "wq_g", "wq_u", "wq_d", "sp_z", "ld_szp",
                 "out"]
    sm = {n: Sem(stack, nc, n) for n in sem_names}

    ps = stack.enter_context(nc.psum_tensor("ps", [128, 4096], F32)).ap()

    def bank(i):
        return ps[:, 512 * i:512 * (i + 1)]

    def bank2(i):
        return ps[:, 512 * i:512 * (i + 2)]

    bank_free = [None] * 8

    gc_ = nc.alloc_sbuf_tensor("g_consts", [128, 64], F32).ap()
    onesc_f = gc_[:, 0:1]
    lp_sb = gc_[:, 8:16]
    dfb = gc_[:, 16:24]
    onesc_b = nc.alloc_sbuf_tensor("onesc_b", [128, 1], BF16).ap()
    onesr = nc.alloc_sbuf_tensor("onesr", [1, 128], F32).ap()
    shifts = nc.alloc_sbuf_tensor("shifts", [128, 1], I32).ap()
    dfrow = nc.alloc_sbuf_tensor("dfrow", [1, 8], F32).ap()
    xsum_gu = nc.alloc_sbuf_tensor("xsum_gu", [32, TLOC], BF16).ap()
    xsum_d = nc.alloc_sbuf_tensor("xsum_d", [86, 512], BF16).ap()

    b.dma("sync", sm["ld_c"], gc_[:, 0:1], ONESF)
    b.dma("sync", sm["ld_c"], lp_sb, LPD)
    b.dma("sync", sm["ld_c"], dfrow, DFD)
    b.dma("sync", sm["ld_c"], onesr, ONESR)
    b.dma("sync", sm["ld_c"], shifts, SHIFTS)
    c_all = sm["ld_c"].v
    t_onescb = b.dma("sync", sm["ld_c"], onesc_b, ONESC)

    b.wait("scalar", sm["ld_c"], c_all)
    t_dfe = b.op("scalar", sm["sc"], "activation", dfrow, dfrow, AF.Exp, scale=-1.0)
    b.wait("tensor", sm["sc"], t_dfe)
    b.wait("tensor", sm["ld_c"], c_all)
    t0 = b.mm(ps[:, 0:8], onesr, dfrow, True, True, sm["mm"])
    b.wait("vector", sm["mm"], t0)
    t_dfb = b.op("vector", sm["ev"], "tensor_copy", dfb, ps[:, 0:8])
    bank_free[0] = (sm["ev"], t_dfb)

    xn_s = ExitStack()
    xn = xn_s.enter_context(nc.sbuf_tensor("xn1", [128, HC * 1024], BF16)).ap()
    s0 = ExitStack()
    xb = s0.enter_context(nc.sbuf_tensor("xb", [128, 2048], F32)).ap()
    xsq = s0.enter_context(nc.sbuf_tensor("xsq", [128, 2048], F32)).ap()
    srow = s0.enter_context(nc.sbuf_tensor("srow", [1, 4096], F32)).ap()
    sel32 = s0.enter_context(nc.sbuf_tensor("sel32", [128, 1024], BF16)).ap()
    t_sel32 = b.dma("sync", sm["ld_c"], sel32, SEL32N)

    xb_free = [None, None]
    t_stat = [None] * HC
    for c in range(HC):
        sl = c % 2
        b.waitp("sync", xb_free[sl])
        t_x = b.dma("sync", sm["ld_x"], xb[:, 1024 * sl:1024 * (sl + 1)], XT[:, c, :])
        b.wait("scalar", sm["ld_x"], t_x)
        if c >= 2:
            b.wait("scalar", sm["mm"], t_stat[c - 2])
        t_sq = b.op("scalar", sm["sc"], "activation",
                    xsq[:, 1024 * sl:1024 * (sl + 1)], xb[:, 1024 * sl:1024 * (sl + 1)],
                    AF.Square)
        b.wait("tensor", sm["sc"], t_sq)
        b.wait("tensor", sm["ld_c"], c_all)
        if c == 0:
            b.waitp("tensor", bank_free[0])
        for h2 in range(2):
            b.mm(ps[0:1, 512 * h2:512 * (h2 + 1)], onesc_f,
                 xb[:, 1024 * sl + 512 * h2:1024 * sl + 512 * (h2 + 1)],
                 c == 0, c == HC - 1, None)
            b.mm(ps[0:1, 1024 + 512 * h2:1024 + 512 * (h2 + 1)], onesc_f,
                 xsq[:, 1024 * sl + 512 * h2:1024 * sl + 512 * (h2 + 1)],
                 c == 0, c == HC - 1, sm["mm"] if h2 == 1 else None)
        t_stat[c] = sm["mm"].v
        xb_free[sl] = (sm["mm"], t_stat[c])

    mean, var = srow[0:1, 0:1024], srow[0:1, 1024:2048]
    arow, brow = srow[0:1, 2048:3072], srow[0:1, 3072:4096]
    b.wait("vector", sm["mm"], t_stat[HC - 1])
    b.op("vector", None, "tensor_scalar", mean, ps[0:1, 0:1024], 1.0 / H, None, ALU.mult)
    b.op("vector", None, "tensor_scalar", var, ps[0:1, 1024:2048], 1.0 / H, None, ALU.mult)
    b.op("vector", None, "tensor_tensor", brow, mean, mean, ALU.mult)
    t_v = b.op("vector", sm["ev"], "tensor_tensor", var, var, brow, ALU.subtract)
    for i in range(4):
        bank_free[i] = (sm["ev"], t_v)
    t_eps = b.op("vector", sm["ev"], "tensor_scalar", var, var, EPS, None, ALU.add)
    b.wait("scalar", sm["ev"], t_eps)
    t_sd = b.op("scalar", sm["sc"], "activation", var, var, AF.Sqrt)
    b.wait("vector", sm["sc"], t_sd)
    b.op("vector", None, "reciprocal", arow, var)
    t_ab = b.op("vector", sm["ev"], "tensor_scalar", brow, mean, -1.0, None, ALU.mult)
    b.wait("tensor", sm["ev"], t_ab)
    for h2 in range(2):
        b.mm(ps[:, 2048 + 512 * h2:2560 + 512 * h2], onesr,
             arow[0:1, 512 * h2:512 * (h2 + 1)], True, True, None)
        b.mm(ps[:, 3072 + 512 * h2:3584 + 512 * h2], onesr,
             brow[0:1, 512 * h2:512 * (h2 + 1)], True, True,
             sm["mm"] if h2 == 1 else None)
    t_bcast = sm["mm"].v
    ps_a, ps_b = ps[:, 2048:3072], ps[:, 3072:4096]

    tmpf = [xsq[:, 0:1024], xsq[:, 1024:2048]]
    b.wait("vector", sm["mm"], t_bcast)
    xb_free = [(sm["mm"], t_stat[HC - 1])] * 2
    t_xn_last = 0
    for c in range(HC):
        sl = c % 2
        b.waitp("sync", xb_free[sl])
        t_x = b.dma("sync", sm["ld_x"], xb[:, 1024 * sl:1024 * (sl + 1)], XT[:, c, :])
        b.wait("vector", sm["ld_x"], t_x)
        xnc = xn[:, 1024 * c:1024 * (c + 1)]
        b.op("vector", None, "tensor_tensor", tmpf[sl],
             xb[:, 1024 * sl:1024 * (sl + 1)], ps_b, ALU.add)
        t_xn = b.op("vector", sm["ev"], "tensor_tensor", xnc, tmpf[sl], ps_a, ALU.mult)
        xb_free[sl] = (sm["ev"], t_xn)
        t_xn_last = t_xn
        b.wait("scalar", sm["ev"], t_xn)
        b.dma("scalar", sm["sp_xn"], XN_SP[:, 1024 * c:1024 * (c + 1)], xnc)
        b.wait("tensor", sm["ev"], t_xn)
        b.wait("tensor", sm["ld_c"], t_sel32)
        if c == 0:
            b.waitp("tensor", bank_free[0])
        for h2 in range(2):
            b.mm(ps[0:32, 512 * h2:512 * (h2 + 1)], sel32[:, 32 * c:32 * (c + 1)],
                 xnc[:, 512 * h2:512 * (h2 + 1)], c == 0, c == HC - 1,
                 sm["mm"] if (c == HC - 1 and h2 == 1) else None)
    t_xsum = sm["mm"].v
    b.wait("vector", sm["mm"], t_xsum)
    t_xs = b.op("vector", sm["ev"], "tensor_copy", xsum_gu, ps[0:32, 0:1024])
    bank_free[0] = bank_free[1] = (sm["ev"], t_xs)
    for i in range(4, 8):
        bank_free[i] = (sm["ev"], t_xn_last)
    sp_xn_all = sm["sp_xn"].v
    s0.close()

    def phase_gate(eng, *dma_sems):
        for s_ in ("mm", "ev", "sc", "gp"):
            b.wait(eng, sm[s_], sm[s_].v)
        for s_ in dma_sems:
            b.wait(eng, sm[s_], sm[s_].v)

    class Deq:
        def __init__(self, dq, which):
            self.which = which
            mat = {"g": 0, "u": 1, "d": 2}[which]
            self.nch = HC if which != "d" else IC
            self.ncols = II if which != "d" else H
            self.nzg = 32 if which != "d" else 86
            self.wq = sm["wq_" + which]
            nzg, ncols = self.nzg, self.ncols

            self.sres = dq.enter_context(
                nc.sbuf_tensor(f"sres_{which}", [nzg, ncols], BF16)).ap()
            self.slo = dq.enter_context(
                nc.sbuf_tensor(f"slo_{which}", [nzg, ncols], BF16)).ap()
            self.t_s = b.dma("gpsimd", sm["ld_s"], self.sres,
                             SGU[mat] if which != "d" else SD)
            t_lo = 0
            with nc.sbuf_tensor(f"stmp_{which}", [nzg, ncols // 2], F32) as stmp_h:
                stmp = stmp_h.ap()
                for hh in range(2):
                    csl = slice((ncols // 2) * hh, (ncols // 2) * (hh + 1))
                    if t_lo:
                        b.wait("sync", sm["ev"], t_lo)
                    t_sf = b.dma("sync", sm["ld_z"], stmp,
                                 (SGU[mat] if which != "d" else SD)[:, csl])
                    b.wait("vector", sm["ld_z"], t_sf)
                    b.wait("vector", sm["ld_s"], self.t_s)
                    t_lo = b.op("vector", sm["ev"], "tensor_tensor",
                                self.slo[:, csl], stmp, self.sres[:, csl],
                                ALU.subtract)
            self.t_lo = t_lo
            zs_scope = ExitStack()
            qz = zs_scope.enter_context(nc.sbuf_tensor(f"qz_{which}", [nzg, ncols // 8], I32)).ap()
            zt = zs_scope.enter_context(nc.sbuf_tensor(f"zt_{which}", [nzg, ncols // 8], I32)).ap()
            zf = zs_scope.enter_context(nc.sbuf_tensor(f"zf_{which}", [nzg, ncols // 8], F32)).ap()
            szp = zs_scope.enter_context(nc.sbuf_tensor(f"szp_{which}", [nzg, ncols], BF16)).ap()
            b.wait("sync", sm["ev"], self.t_lo)
            t_qz = b.dma("sync", sm["ld_z"], qz, QZGU[mat] if which != "d" else QZD)
            b.wait("vector", sm["ld_z"], t_qz)
            b.wait("vector", sm["ld_s"], self.t_s)
            t_z = 0
            for jj in range(8):
                t_sh = b.op("vector", sm["ev"], "tensor_scalar", zt, qz, 4 * jj, 15,
                            ALU.logical_shift_right, ALU.bitwise_and)
                b.wait("gpsimd", sm["ev"], t_sh)
                t_zf = b.op("gpsimd", sm["gp"], "tensor_copy", zf, zt)
                b.wait("vector", sm["gp"], t_zf)
                dst = szp.rearrange("g (rh rl) -> g rh rl", rl=128)[:, :, 16 * jj:16 * jj + 16]
                ssrc = self.sres.rearrange("g (rh rl) -> g rh rl", rl=128)[:, :, 16 * jj:16 * jj + 16]
                zsrc = zf.rearrange("g (rh rl) -> g rh rl", rl=16)
                t_z = b.op("vector", sm["ev"], "scalar_tensor_tensor", dst, zsrc,
                           -7.0, ssrc, ALU.add, ALU.mult)
            b.wait("scalar", sm["ev"], t_z)
            self.t_szp_sp = b.dma("scalar", sm["sp_z"],
                                  SZP_GU[:, mat * II:(mat + 1) * II] if which != "d" else SZP_D, szp)
            zs_scope.close()

            self.qb = dq.enter_context(nc.sbuf_tensor(f"qb_{which}", [128, 2048], I32)).ap()
            self.w4i = dq.enter_context(nc.sbuf_tensor(f"w4i_{which}", [128, 1024], I32)).ap()
            self.w4f = dq.enter_context(nc.sbuf_tensor(f"w4f_{which}", [128, 1024], F32)).ap()
            self.wo = dq.enter_context(nc.sbuf_tensor(f"wo_{which}", [128, 2048], BF16)).ap()
            self.bcb = dq.enter_context(
                nc.sbuf_tensor(f"bcb_{which}", [nzg, 256], BF16)).ap()
            self.strip = lambda c: (QGU[c, mat] if which != "d" else QD[c])
            self.wdst = lambda c: (WDQ_GU[c, mat] if which != "d" else WDQ_D[c])
            self.bc_src = (BC32 if which != "d" else BC86)
            self.gr = [(c, g0, min(1024, ncols - g0))
                       for c in range(self.nch) for g0 in range(0, ncols, 1024)]
            self.i = 0
            self.qb_free = [None, None]
            self.wo_free = [None, None]
            self.bc_free = [0, 0]
            self.sb_free = [bank_free[4], bank_free[6]]
            self.gp_prev = 0
            self.mu_prev = 0
            self.t_bc = 0
            self.cur_c = -1

        def done(self):
            return self.i >= len(self.gr)

        def step(self, n):
            for _ in range(n):
                if self.done():
                    return
                c, g0, gcols = self.gr[self.i]
                sl = self.i % 2
                if c != self.cur_c:
                    self.cur_c = c
                    bsl = c % 2
                    self.bsl = bsl
                    b.wait("gpsimd", sm["mm"], self.bc_free[bsl])
                    self.t_bc = b.dma("gpsimd", sm["ld_bc"],
                                      self.bcb[0:self.nzg, 128 * bsl:128 * (bsl + 1)],
                                      self.bc_src[c])
                b.waitp("sync", self.qb_free[sl])
                b.wait("sync", sm["sp_z"], self.t_szp_sp)
                t_q = b.dma("sync", sm["ld_q"], self.qb[:, 1024 * sl:1024 * sl + gcols],
                            self.strip(c)[:, g0:g0 + gcols])
                psl = 4 + 2 * sl
                b.waitp("tensor", self.sb_free[sl])
                b.wait("tensor", sm["ld_bc"], self.t_bc)
                b.wait("tensor", sm["ld_s"], self.t_s)
                b.wait("tensor", sm["ev"], self.t_lo)
                t_sb = None
                for q0 in range(0, gcols, 512):
                    qc = min(512, gcols - q0)
                    bcs = self.bcb[0:self.nzg, 128 * self.bsl:128 * (self.bsl + 1)]
                    b.mm(ps[:, 512 * psl + q0:512 * psl + q0 + qc], bcs,
                         self.sres[0:self.nzg, g0 + q0:g0 + q0 + qc],
                         True, False, None)
                    t_sb = b.mm(ps[:, 512 * psl + q0:512 * psl + q0 + qc], bcs,
                                self.slo[0:self.nzg, g0 + q0:g0 + q0 + qc],
                                False, True,
                                sm["mm"] if q0 + 512 >= gcols else None)
                b.wait("vector", sm["ld_q"], t_q)
                b.wait("vector", sm["gp"], self.gp_prev)
                t_sh = b.op("vector", sm["ev"], "tensor_scalar",
                            self.w4i[:, 0:gcols], self.qb[:, 1024 * sl:1024 * sl + gcols],
                            shifts, 15, ALU.logical_shift_right, ALU.bitwise_and)
                self.qb_free[sl] = (sm["ev"], t_sh)
                b.wait("gpsimd", sm["ev"], max(t_sh, self.mu_prev))
                self.gp_prev = b.op("gpsimd", sm["gp"], "tensor_copy",
                                    self.w4f[:, 0:gcols], self.w4i[:, 0:gcols])
                b.wait("vector", sm["gp"], self.gp_prev)
                b.wait("vector", sm["mm"], t_sb)
                b.waitp("vector", self.wo_free[sl])
                self.mu_prev = b.op("vector", sm["ev"], "scalar_tensor_tensor",
                                    self.wo[:, 1024 * sl:1024 * sl + gcols],
                                    self.w4f[:, 0:gcols], -8.0,
                                    ps[:, 512 * psl:512 * psl + gcols],
                                    ALU.add, ALU.mult)
                self.sb_free[sl] = (sm["ev"], self.mu_prev)
                self.bc_free[self.bsl] = t_sb
                b.wait("scalar", sm["ev"], self.mu_prev)
                twq = b.dma("scalar", self.wq, self.wdst(c)[:, g0:g0 + gcols],
                            self.wo[:, 1024 * sl:1024 * sl + gcols])
                self.wo_free[sl] = (self.wq, twq)
                self.i += 1

        def finish(self):
            self.step(len(self.gr))
            bank_free[4] = bank_free[5] = self.sb_free[0]
            bank_free[6] = bank_free[7] = self.sb_free[1]

    s1 = ExitStack()
    win = s1.enter_context(nc.sbuf_tensor("win", [128, 2 * HC * 128], BF16)).ap()
    au_f = s1.enter_context(nc.sbuf_tensor("au_f", [128, 1024], F32)).ap()
    delta = s1.enter_context(nc.sbuf_tensor("delta", [128, 1024], F32)).ap()
    keep = s1.enter_context(nc.sbuf_tensor("keep", [128, 1024], F32)).ap()
    qpow = s1.enter_context(nc.sbuf_tensor("qpow", [128, 1024], F32)).ap()
    tser = s1.enter_context(nc.sbuf_tensor("tser", [128, 1024], F32)).ap()
    tbf = s1.enter_context(nc.sbuf_tensor("tbf", [128, 2048], BF16)).ap()
    phase_gate("sync", "sp_xn")
    phase_gate("vector")
    phase_gate("gpsimd")
    dq_g = ExitStack()
    deq_g = Deq(dq_g, "g")

    win_free = [None, None]
    tbf_free = [None, None]
    ser_free = None
    for s in range(SC):
        for half in range(2):
            o = 2 * s + half
            sl = o % 2
            b.waitp("gpsimd", win_free[sl])
            t_w = b.dma("gpsimd", sm["ld_win"],
                        win[:, sl * HC * 128:(sl + 1) * HC * 128], W_IN[o])
            b.wait("tensor", sm["ld_win"], t_w)
            for h2 in range(2):
                psl = 2 * half + h2
                b.waitp("tensor", bank_free[psl])
                for k in range(HC):
                    b.mm(bank(psl),
                         win[:, sl * HC * 128 + 128 * k:sl * HC * 128 + 128 * (k + 1)],
                         xn[:, 1024 * k + 512 * h2:1024 * k + 512 * (h2 + 1)],
                         k == 0, k == HC - 1, sm["mm"] if k == HC - 1 else None)
            win_free[sl] = (sm["mm"], sm["mm"].v)
        t_gate_grp = sm["mm"].v
        t_up_grp = t_gate_grp - 2
        b.wait("scalar", sm["mm"], t_up_grp)
        b.waitp("scalar", ser_free)
        t_auf = b.op("scalar", sm["sc"], "activation", au_f, ps[:, 0:1024], AF.Copy)
        b.wait("vector", sm["sc"], t_auf)
        b.waitp("vector", tbf_free[0])
        t_aubf = b.op("vector", sm["ev"], "tensor_copy", tbf[:, 0:1024], au_f)
        bank_free[0] = bank_free[1] = (sm["ev"], t_aubf)
        b.wait("scalar", sm["ev"], t_aubf)
        tsp = b.dma("scalar", sm["sp_aut"], AUT_SP[:, 1024 * s:1024 * (s + 1)],
                    tbf[:, 0:1024])
        tbf_free[0] = (sm["sp_aut"], tsp)
        b.wait("vector", sm["mm"], t_gate_grp)
        t_dl = b.op("vector", sm["ev"], "tensor_scalar", delta, ps[:, 1024:2048],
                    lp_sb[:, s:s + 1], None, ALU.subtract)
        bank_free[2] = bank_free[3] = (sm["ev"], t_dl)
        b.wait("scalar", sm["ev"], t_dl)
        t_abs = b.op("scalar", sm["sc"], "activation", tser, delta, AF.Abs)
        b.wait("vector", sm["sc"], t_abs)
        b.op("vector", None, "tensor_scalar", keep, tser, THD, None, ALU.is_le)
        t_t = None
        for m in range(1, G + 1):
            src_q = delta if m == 1 else qpow
            b.op("vector", None, "tensor_tensor", tser, src_q, au_f, ALU.mult)
            tsl = m % 2
            b.waitp("vector", tbf_free[tsl])
            dst = tbf[:, 1024 * tsl:1024 * (tsl + 1)]
            if m > 4:
                b.op("vector", None, "tensor_scalar", tser, tser,
                     dfb[:, m - 1:m], None, ALU.mult)
                t_t = b.op("vector", sm["ev"], "tensor_tensor", dst, tser, keep, ALU.mult)
            else:
                t_t = b.op("vector", sm["ev"], "tensor_scalar", dst, tser,
                           dfb[:, m - 1:m], None, ALU.mult)
            b.wait("scalar", sm["ev"], t_t)
            tsp = b.dma("scalar", sm["sp_aut"],
                        AUT_SP[:, 1024 * (8 * m + s):1024 * (8 * m + s + 1)], dst)
            tbf_free[tsl] = (sm["sp_aut"], tsp)
            if m < G:
                b.op("vector", None, "tensor_tensor", qpow, src_q, delta, ALU.mult)
        ser_free = (sm["ev"], t_t)
        deq_g.step(43)
    deq_g.finish()
    sp_aut_all = sm["sp_aut"].v
    dq_g.close()
    s1.close()
    xn_s.close()

    s2 = ExitStack()
    aut = s2.enter_context(nc.sbuf_tensor("aut", [128, 56 * 1024], BF16)).ap()
    w1b = s2.enter_context(nc.sbuf_tensor("w1b", [128, 2 * 56 * 128], BF16)).ap()
    pab = s2.enter_context(nc.sbuf_tensor("pab", [128, 2 * 1024], F32)).ap()
    phase_gate("gpsimd", "wq_g", "sp_xn")
    phase_gate("sync", "wq_g", "sp_xn", "sp_z", "sp_aut")
    phase_gate("vector", "wq_g")
    dq_d = ExitStack()
    deq_d = Deq(dq_d, "d")

    b.wait("sync", sm["sp_aut"], sp_aut_all)
    phase_gate("sync", "wq_g", "sp_xn", "sp_z")
    t_aut = b.dma("sync", sm["ld_aut"], aut, AUT_SP)
    w1_free = [None, None]
    pab_free = [None, None]
    for o in range(HC):
        sl = o % 2
        b.waitp("gpsimd", w1_free[sl])
        t_w = b.dma("gpsimd", sm["ld_w1"],
                    w1b[:, sl * 56 * 128:(sl + 1) * 56 * 128], W_OUT1[o])
        b.wait("tensor", sm["ld_w1"], t_w)
        b.wait("tensor", sm["ld_aut"], t_aut)
        for h2 in range(2):
            psl = 2 * (o % 2) + h2
            b.waitp("tensor", bank_free[psl])
            for k in range(56):
                b.mm(bank(psl),
                     w1b[:, sl * 56 * 128 + 128 * k:sl * 56 * 128 + 128 * (k + 1)],
                     aut[:, 1024 * k + 512 * h2:1024 * k + 512 * (h2 + 1)],
                     k == 0, k == 55, sm["mm"] if k == 55 else None)
        t_grp = sm["mm"].v
        w1_free[sl] = (sm["mm"], t_grp)
        b.wait("scalar", sm["mm"], t_grp)
        b.waitp("scalar", pab_free[sl])
        t_pa = b.op("scalar", sm["sc"], "activation",
                    pab[:, 1024 * sl:1024 * (sl + 1)], bank2(2 * (o % 2)), AF.Copy)
        bank_free[2 * (o % 2)] = bank_free[2 * (o % 2) + 1] = (sm["sc"], t_pa)
        tsp = b.dma("scalar", sm["sp_pa"], PA[:, o, :], pab[:, 1024 * sl:1024 * (sl + 1)])
        pab_free[sl] = (sm["sp_pa"], tsp)
        deq_d.step(11)
    deq_d.finish()
    if DBG:
        DBG_AUT = dscr("DBG_AUT", (128, 56 * 1024), BF16)
        DBG_W1B = dscr("DBG_W1B", (128, 2 * 56 * 128), BF16)
        b.wait("sync", sm["mm"], sm["mm"].v)
        b.dma("sync", sm["ld_c"], DBG_AUT, aut)
        b.dma("sync", sm["ld_c"], DBG_W1B, w1b)
        b.wait("sync", sm["ld_c"], sm["ld_c"].v)
    dq_d.close()
    s2.close()

    s3 = ExitStack()
    xn3 = s3.enter_context(nc.sbuf_tensor("xn3", [128, HC * 1024], BF16)).ap()
    wgb = s3.enter_context(nc.sbuf_tensor("wgb", [128, 2 * HC * 128], BF16)).ap()
    sil = s3.enter_context(nc.sbuf_tensor("sil", [128, 1024], F32)).ap()
    hab = s3.enter_context(nc.sbuf_tensor("hab", [128, 2048], BF16)).ap()
    phase_gate("gpsimd", "wq_d", "sp_pa")
    phase_gate("sync", "wq_d", "sp_pa", "sp_z")
    phase_gate("vector", "wq_d")
    dq_u = ExitStack()
    deq_u = Deq(dq_u, "u")

    b.wait("sync", sm["sp_xn"], sp_xn_all)
    phase_gate("sync", "wq_d", "sp_pa", "sp_z")
    t_xn3 = b.dma("sync", sm["ld_xn"], xn3, XN_SP)
    wg_free = [None, None]
    hab_free = [None, None]
    sil_free = None
    for cf in range(FC):
        for half in range(2):
            o = 2 * cf + half
            sl = o % 2
            b.waitp("gpsimd", wg_free[sl])
            t_w = b.dma("gpsimd", sm["ld_agu"],
                        wgb[:, sl * HC * 128:(sl + 1) * HC * 128], W_AGU[o])
            b.wait("tensor", sm["ld_agu"], t_w)
            b.wait("tensor", sm["ld_xn"], t_xn3)
            for h2 in range(2):
                psl = 2 * half + h2
                b.waitp("tensor", bank_free[psl])
                for k in range(HC):
                    b.mm(bank(psl),
                         wgb[:, sl * HC * 128 + 128 * k:sl * HC * 128 + 128 * (k + 1)],
                         xn3[:, 1024 * k + 512 * h2:1024 * k + 512 * (h2 + 1)],
                         k == 0, k == HC - 1, sm["mm"] if k == HC - 1 else None)
            wg_free[sl] = (sm["mm"], sm["mm"].v)
        t_up = sm["mm"].v
        t_gate = t_up - 2
        b.wait("scalar", sm["mm"], t_gate)
        b.waitp("scalar", sil_free)
        t_sil = b.op("scalar", sm["sc"], "activation", sil, ps[:, 0:1024], AF.Silu)
        bank_free[0] = bank_free[1] = (sm["sc"], t_sil)
        b.wait("vector", sm["sc"], t_sil)
        b.wait("vector", sm["mm"], t_up)
        hsl = cf % 2
        b.waitp("vector", hab_free[hsl])
        t_h = b.op("vector", sm["ev"], "tensor_tensor",
                   hab[:, 1024 * hsl:1024 * (hsl + 1)], sil, ps[:, 1024:2048], ALU.mult)
        sil_free = (sm["ev"], t_h)
        bank_free[2] = bank_free[3] = (sm["ev"], t_h)
        b.wait("scalar", sm["ev"], t_h)
        tsp = b.dma("scalar", sm["sp_ha"], HA_SP[:, 1024 * cf:1024 * (cf + 1)],
                    hab[:, 1024 * hsl:1024 * (hsl + 1)])
        hab_free[hsl] = (sm["sp_ha"], tsp)
        deq_u.step(5)
    deq_u.finish()
    sp_ha_all = sm["sp_ha"].v
    dq_u.close()
    s3.close()

    s4 = ExitStack()
    ha = s4.enter_context(nc.sbuf_tensor("ha", [128, FC * 1024], BF16)).ap()
    wdb = s4.enter_context(nc.sbuf_tensor("wdb", [128, 2 * FC * 128], BF16)).ap()
    pbb = s4.enter_context(nc.sbuf_tensor("pbb", [128, 2048], F32)).ap()
    b.wait("sync", sm["sp_ha"], sp_ha_all)
    phase_gate("sync", "wq_u", "sp_z")
    phase_gate("gpsimd", "wq_u")
    t_ha = b.dma("sync", sm["ld_ha"], ha, HA_SP)
    wd_free = [None, None]
    pbb_free = [None, None]
    for o in range(HC):
        sl = o % 2
        b.waitp("gpsimd", wd_free[sl])
        t_w = 0
        for spi in range(2):
            t_w = b.dma("gpsimd", sm["ld_ad"],
                        wdb[:, sl * FC * 128 + spi * 39 * 128:
                            sl * FC * 128 + (spi + 1) * 39 * 128],
                        W_AD[o][:, spi * 39 * 128:(spi + 1) * 39 * 128])
        b.wait("tensor", sm["ld_ad"], t_w)
        b.wait("tensor", sm["ld_ha"], t_ha)
        for h2 in range(2):
            psl = 2 * (o % 2) + h2
            b.waitp("tensor", bank_free[psl])
            for k in range(FC):
                b.mm(bank(psl),
                     wdb[:, sl * FC * 128 + 128 * k:sl * FC * 128 + 128 * (k + 1)],
                     ha[:, 1024 * k + 512 * h2:1024 * k + 512 * (h2 + 1)],
                     k == 0, k == FC - 1, sm["mm"] if k == FC - 1 else None)
        t_grp = sm["mm"].v
        wd_free[sl] = (sm["mm"], t_grp)
        b.wait("scalar", sm["mm"], t_grp)
        b.waitp("scalar", pbb_free[sl])
        t_pb = b.op("scalar", sm["sc"], "activation",
                    pbb[:, 1024 * sl:1024 * (sl + 1)], bank2(2 * (o % 2)), AF.Copy)
        bank_free[2 * (o % 2)] = bank_free[2 * (o % 2) + 1] = (sm["sc"], t_pb)
        tsp = b.dma("scalar", sm["sp_pb"], PB[:, o, :], pbb[:, 1024 * sl:1024 * (sl + 1)])
        pbb_free[sl] = (sm["sp_pb"], tsp)
    s4.close()

    s5 = ExitStack()
    xn5 = s5.enter_context(nc.sbuf_tensor("xn5", [128, HC * 1024], BF16)).ap()
    w5b = s5.enter_context(nc.sbuf_tensor("w5b", [128, 4 * HC * 256], BF16)).ap()
    szp_gu5 = s5.enter_context(nc.sbuf_tensor("szp_gu5", [32, 2 * II], BF16)).ap()
    sil5 = s5.enter_context(nc.sbuf_tensor("sil5", [128, 1024], F32)).ap()
    hgb = s5.enter_context(nc.sbuf_tensor("hgb", [128, 2048], BF16)).ap()
    b.wait("sync", sm["sp_xn"], sp_xn_all)
    phase_gate("sync", "sp_pb")
    t_xn5 = b.dma("sync", sm["ld_xn"], xn5, XN_SP)
    b.wait("sync", sm["sp_z"], sm["sp_z"].v)
    t_szp = b.dma("sync", sm["ld_szp"], szp_gu5, SZP_GU)
    wq_g_all, wq_u_all = sm["wq_g"].v, sm["wq_u"].v

    quad_free = [None, None]
    sil5_free = None
    hgb_free = [None, None]
    for q in range(IC // 2):
        sl = q % 2
        b.waitp("sync", quad_free[sl])
        b.wait("sync", sm["wq_g"], wq_g_all)
        b.wait("sync", sm["wq_u"], wq_u_all)
        t_w = [0, 0]
        for mi in range(2):
            src = WDQ_GU[:, mi, :, 256 * q:256 * (q + 1)].transpose([1, 0, 2])
            dst = w5b[:, (2 * sl + mi) * HC * 256:(2 * sl + mi + 1) * HC * 256] \
                .rearrange("p (kc m) -> p kc m", m=256)
            t_w[mi] = b.dma("sync", sm["ld_w5"], dst, src)
        grp_t = {}
        for sub in range(2):
            cf = 2 * q + sub
            for mi in range(2):
                base = (2 * sl + mi) * HC * 256 + sub * 128
                b.wait("tensor", sm["ld_w5"], t_w[mi])
                b.wait("tensor", sm["ld_xn"], t_xn5)
                b.wait("tensor", sm["ld_szp"], t_szp)
                for h2 in range(2):
                    psl = 4 * (q % 2) + 2 * mi + h2
                    b.waitp("tensor", bank_free[psl])
                    for k in range(HC):
                        b.mm(bank(psl), w5b[:, base + 256 * k:base + 256 * k + 128],
                             xn5[:, 1024 * k + 512 * h2:1024 * k + 512 * (h2 + 1)],
                             k == 0, False, None)
                    b.mm(bank(psl), szp_gu5[:, mi * II + 128 * cf:mi * II + 128 * (cf + 1)],
                         xsum_gu[0:32, 512 * h2:512 * (h2 + 1)],
                         False, True, sm["mm"])
                grp_t[mi] = sm["mm"].v
            pb0 = 4 * (q % 2)
            b.wait("scalar", sm["mm"], grp_t[0])
            b.waitp("scalar", sil5_free)
            t_sil = b.op("scalar", sm["sc"], "activation", sil5,
                         ps[:, 512 * pb0:512 * pb0 + 1024], AF.Silu)
            bank_free[pb0] = bank_free[pb0 + 1] = (sm["sc"], t_sil)
            b.wait("vector", sm["sc"], t_sil)
            b.wait("vector", sm["mm"], grp_t[1])
            hsl = cf % 2
            b.waitp("vector", hgb_free[hsl])
            t_h = b.op("vector", sm["ev"], "tensor_tensor",
                       hgb[:, 1024 * hsl:1024 * (hsl + 1)], sil5,
                       ps[:, 512 * (pb0 + 2):512 * (pb0 + 2) + 1024], ALU.mult)
            sil5_free = (sm["ev"], t_h)
            bank_free[pb0 + 2] = bank_free[pb0 + 3] = (sm["ev"], t_h)
            tsp = 0
            b.wait("scalar", sm["ev"], t_h)
            for h2 in range(2):
                tsp = b.dma("scalar", sm["sp_hg"], HG_SP[:, h2, cf, :],
                            hgb[:, 1024 * hsl + 512 * h2:1024 * hsl + 512 * (h2 + 1)])
            hgb_free[hsl] = (sm["sp_hg"], tsp)
        quad_free[sl] = (sm["mm"], sm["mm"].v)
    sp_hg_all = sm["sp_hg"].v
    s5.close()

    s6 = ExitStack()
    hg = s6.enter_context(nc.sbuf_tensor("hg", [128, IC * 512], BF16)).ap()
    w6b = s6.enter_context(nc.sbuf_tensor("w6b", [128, 2 * IC * 128], BF16)).ap()
    szp_d6 = s6.enter_context(nc.sbuf_tensor("szp_d6", [86, H], BF16)).ap()
    sel86 = s6.enter_context(nc.sbuf_tensor("sel86", [128, 2 * 86], BF16)).ap()
    resb = s6.enter_context(nc.sbuf_tensor("resb", [128, 6 * 512], F32)).ap()
    tadd = s6.enter_context(nc.sbuf_tensor("tadd", [128, 512], F32)).ap()
    obuf = s6.enter_context(nc.sbuf_tensor("obuf", [128, 2 * 512], F32)).ap()
    b.wait("sync", sm["sp_z"], sm["sp_z"].v)
    t_szd = b.dma("sync", sm["ld_szp"], szp_d6, SZP_D)
    wq_d_all = sm["wq_d"].v

    hg_ld = None
    phase_gate("sync")
    phase_gate("gpsimd")
    for half in range(2):
        b.wait("sync", sm["sp_hg"], sp_hg_all)
        if hg_ld is not None:
            b.wait("sync", sm["mm"], hg_ld)
        t_hg = b.dma("sync", sm["ld_hg"], hg,
                     HG_SP[:, half, :, :].rearrange("p c t -> p (c t)")
                     if False else HG_SP[:, half])
        selb_free = None
        b.waitp("tensor", bank_free[4])
        for c in range(IC):
            ssl = c % 2
            b.waitp("gpsimd", selb_free)
            t_sel = b.dma("gpsimd", sm["ld_sel"], sel86[:, 86 * ssl:86 * (ssl + 1)],
                          SEL86N[c])
            b.wait("tensor", sm["ld_sel"], t_sel)
            b.wait("tensor", sm["ld_hg"], t_hg)
            t_pm = b.mm(ps[0:86, 2048:2560], sel86[:, 86 * ssl:86 * (ssl + 1)],
                        hg[:, 512 * c:512 * (c + 1)], c == 0, c == IC - 1, sm["mm"])
            selb_free = (sm["mm"], t_pm)
        t_xsd = sm["mm"].v
        b.wait("vector", sm["mm"], t_xsd)
        t_xsde = b.op("vector", sm["ev"], "tensor_copy", xsum_d, ps[0:86, 2048:2560])
        bank_free[4] = (sm["ev"], t_xsde)

        w6_free = [None, None]
        res_free = [None, None]
        obuf_free = [None, None]
        for o in range(HC):
            sl = o % 2
            b.waitp("sync", res_free[sl])
            b.wait("sync", sm["sp_pa"], sm["sp_pa"].v)
            b.wait("sync", sm["sp_pb"], sm["sp_pb"].v)
            tr = b.dma("sync", sm["ld_pax"], resb[:, 512 * sl:512 * (sl + 1)],
                       PA[:, o, 512 * half:512 * (half + 1)])
            tr = b.dma("sync", sm["ld_pax"], resb[:, 1024 + 512 * sl:1536 + 512 * sl],
                       PB[:, o, 512 * half:512 * (half + 1)])
            tr = b.dma("sync", sm["ld_pax"], resb[:, 2048 + 512 * sl:2560 + 512 * sl],
                       XT[:, o, 512 * half:512 * (half + 1)])
            b.waitp("sync", w6_free[sl])
            b.wait("sync", sm["wq_d"], wq_d_all)
            src = WDQ_D[:, :, 128 * o:128 * (o + 1)].transpose([1, 0, 2])
            dst = w6b[:, sl * IC * 128:(sl + 1) * IC * 128] \
                .rearrange("p (c m) -> p c m", m=128)
            t_w6a = b.dma("sync", sm["ld_w6"],
                          dst[:, 0:43, :], src[:, 0:43, :])
            t_w6 = b.dma("sync", sm["ld_w6"],
                         dst[:, 43:86, :], src[:, 43:86, :])
            psl = o % 2
            b.wait("tensor", sm["ld_w6"], t_w6)
            b.wait("tensor", sm["ld_hg"], t_hg)
            b.wait("tensor", sm["ld_szp"], t_szd)
            b.wait("tensor", sm["ev"], t_xsde)
            b.waitp("tensor", bank_free[psl])
            for k in range(IC):
                b.mm(bank(psl),
                     w6b[:, sl * IC * 128 + 128 * k:sl * IC * 128 + 128 * (k + 1)],
                     hg[:, 512 * k:512 * (k + 1)], k == 0, False, None)
            b.mm(bank(psl), szp_d6[0:86, 128 * o:128 * (o + 1)], xsum_d[0:86, :],
                 False, True, sm["mm"])
            t_grp = sm["mm"].v
            w6_free[sl] = (sm["mm"], t_grp)
            b.wait("vector", sm["mm"], t_grp)
            b.wait("vector", sm["ld_pax"], tr)
            b.op("vector", None, "tensor_tensor", tadd, bank(psl),
                 resb[:, 512 * sl:512 * (sl + 1)], ALU.add)
            b.op("vector", None, "tensor_tensor", tadd, tadd,
                 resb[:, 1024 + 512 * sl:1536 + 512 * sl], ALU.add)
            b.waitp("vector", obuf_free[sl])
            t_o = b.op("vector", sm["ev"], "tensor_tensor", obuf[:, 512 * sl:512 * (sl + 1)],
                       tadd, resb[:, 2048 + 512 * sl:2560 + 512 * sl], ALU.add)
            bank_free[psl] = (sm["ev"], t_o)
            res_free[sl] = (sm["ev"], t_o)
            b.wait("scalar", sm["ev"], t_o)
            t_out = b.dma("scalar", sm["out"], OUT[:, o, 512 * half:512 * (half + 1)],
                          obuf[:, 512 * sl:512 * (sl + 1)])
            obuf_free[sl] = (sm["out"], t_out)
        hg_ld = sm["mm"].v
    s6.close()

    b.wait("scalar", sm["out"], sm["out"].v)
    b.wait("sync", sm["out"], sm["out"].v)

    if DEBUG_MEM:
        print("sbuf bytes remaining:", nc.sbuf_bytes_remaining)

    blk = stack.enter_context(nc.Block())
    for eng in ENGINES:
        fns = b.th[eng]

        def runner(e, fns=fns):
            for f in fns:
                f(e)
        getattr(blk, eng)(runner)
    stack.close()
    return nc, DT



def _tile_stationary(WT):
    K, O = WT.shape
    kc, oc = K // 128, O // 128
    t = WT.reshape(kc, 128, oc, 128).transpose(2, 1, 0, 3)
    return np.ascontiguousarray(t.reshape(oc, 128, kc * 128))


def prep_inputs(inp):
    a = {}
    f32 = np.float32

    up_w = np.asarray(inp["L0_attn_up_w"], f32)
    gate_w = np.asarray(inp["L0_attn_gate_w"], f32)
    lao = np.asarray(inp["L0_attn_local_approx_output"], f32)
    fuse = np.asarray(inp["L0_attn_fuse_weight"], f32)
    ffg = np.asarray(inp["L0_attn_ffn_gate_w"], f32)
    ffu = np.asarray(inp["L0_attn_ffn_up_w"], f32)
    ffd = np.asarray(inp["L0_attn_ffn_down_w"], f32)

    WT_all = np.concatenate([up_w.T, gate_w.T], axis=1)[DEV_H]
    cols = np.concatenate([np.concatenate(
        [np.arange(128 * s, 128 * (s + 1)), np.arange(S + 128 * s, S + 128 * (s + 1))])
        for s in range(SC)])
    a["W_IN"] = _tile_stationary(np.ascontiguousarray(WT_all[:, cols]))

    V = np.concatenate([lao.T, fuse.transpose(2, 1, 0).reshape(G * S, H)], axis=0)
    a["W_OUT1"] = _tile_stationary(np.ascontiguousarray(V[:, DEV_H]))

    AG = np.concatenate([ffg.T, ffu.T], axis=1)[DEV_H]
    cols = np.concatenate([np.concatenate(
        [np.arange(128 * c, 128 * (c + 1)), np.arange(FF + 128 * c, FF + 128 * (c + 1))])
        for c in range(FC)])
    a["W_AGU"] = _tile_stationary(np.ascontiguousarray(AG[:, cols]))

    a["W_AD"] = _tile_stationary(np.ascontiguousarray(ffd.T[:, DEV_H]))

    qg = np.asarray(inp["L0_ffn_gate_qweight"], np.int32)
    qu = np.asarray(inp["L0_ffn_up_qweight"], np.int32)
    DEV_I_ = _dev_order(II)
    rows_idx = 16 * np.arange(HC)[:, None] + (np.arange(128) % 16)[None, :]
    qgu = np.stack([qg[:, DEV_I_], qu[:, DEV_I_]], axis=0)
    a["QGU"] = np.ascontiguousarray(
        qgu[:, rows_idx, :].transpose(1, 0, 2, 3)
    ).reshape(HC, 2, 128, II)
    sg = np.asarray(inp["L0_ffn_gate_scales"], f32)[:, DEV_I_]
    su = np.asarray(inp["L0_ffn_up_scales"], f32)[:, DEV_I_]
    a["SGU"] = np.ascontiguousarray(np.stack([sg, su], axis=0))
    a["QZGU"] = np.ascontiguousarray(np.stack(
        [np.asarray(inp["L0_ffn_gate_qzeros"], np.int32),
         np.asarray(inp["L0_ffn_up_qzeros"], np.int32)], axis=0))

    qd = np.asarray(inp["L0_ffn_down_qweight"], np.int32)
    rows_idx_d = 16 * np.arange(IC)[:, None] + (np.arange(128) % 16)[None, :]
    a["QD"] = np.ascontiguousarray(qd[:, DEV_H][rows_idx_d, :])
    a["SD"] = np.ascontiguousarray(np.asarray(inp["L0_ffn_down_scales"], f32)[:, DEV_H])
    a["QZD"] = np.ascontiguousarray(np.asarray(inp["L0_ffn_down_qzeros"], np.int32))

    lp = np.asarray(inp["L0_attn_local_point"], f32)
    a["LPD"] = np.ascontiguousarray(lp.reshape(8, 128).T)
    df = np.zeros((1, 8), f32)
    df[0, :G] = np.asarray(inp["L0_attn_discount_factor"], f32)
    a["DFD"] = df

    bf = ml_dtypes.bfloat16
    s32 = np.zeros((128, 32, 32), bf)
    for c in range(32):
        s32[:, c, c] = -1.0
    a["SEL32N"] = np.ascontiguousarray(s32.reshape(128, 1024))
    s86 = np.zeros((86, 128, 86), bf)
    for c in range(86):
        s86[c, :, c] = -1.0
    a["SEL86N"] = s86
    bc32 = np.zeros((32, 32, 128), bf)
    for g in range(32):
        bc32[g, g, :] = 1.0
    a["BC32"] = bc32
    bc86 = np.zeros((86, 86, 128), bf)
    for g in range(86):
        bc86[g, g, :] = 1.0
    a["BC86"] = bc86
    a["ONESC"] = np.ones((128, 1), bf)
    a["ONESF"] = np.ones((128, 1), f32)
    a["ONESR"] = np.ones((1, 128), f32)
    a["SHIFTS"] = (4 * (np.arange(128, dtype=np.int32) // 16)).reshape(128, 1)

    x = np.asarray(inp["L0_attn_hidden_states"], f32).reshape(8192, H)
    xdev = np.ascontiguousarray(x.T)[DEV_H]
    xts = []
    for c in range(NCORES):
        blkx = xdev[:, TLOC * c:TLOC * (c + 1)].reshape(HC, 128, TLOC)
        xts.append(np.ascontiguousarray(blkx.transpose(1, 0, 2)))
    return a, xts


_CACHE = {}
LAST_RES = None


def kernel(**inputs):
    global LAST_RES
    arrs, xts = prep_inputs(inputs)
    if "nc" not in _CACHE:
        _CACHE["nc"], _CACHE["dt"] = build_program()
    nc = _CACHE["nc"]
    in_maps = [dict(arrs, XT=xts[c]) for c in range(NCORES)]
    res = run_bass_kernel_spmd(nc, in_maps, core_ids=list(range(NCORES)))
    LAST_RES = res
    odev = np.empty((H, 8192), np.float32)
    o32 = odev.reshape(HC, 128, 8192)
    for c in range(NCORES):
        blko = np.asarray(res.results[c]["OUT"])
        o32[:, PERM, TLOC * c:TLOC * (c + 1)] = blko.transpose(1, 0, 2)
    return np.ascontiguousarray(odev.T).reshape(2, 4096, H)


# version 03
# speedup vs baseline: 1.0771x; 1.0771x over previous
import numpy as np
import ml_dtypes
from contextlib import ExitStack

import concourse.bass as bass
import concourse.mybir as mybir
from concourse.bass_utils import run_bass_kernel_spmd

F32 = mybir.dt.float32
BF16 = mybir.dt.bfloat16
I32 = mybir.dt.int32
AF = mybir.ActivationFunctionType
ALU = mybir.AluOpType

H, S, FF, II, G = 4096, 1024, 9984, 11008, 6
HC, SC, FC, IC = H // 128, S // 128, FF // 128, II // 128
NCORES, TLOC = 8, 1024
THD, EPS = 2.5, 1e-5

PERM = np.array([8 * (p % 16) + p // 16 for p in range(128)])


def _dev_order(n):
    j = np.arange(n)
    return 128 * (j // 128) + PERM[j % 128]


DEV_H = _dev_order(H)

ENGINES = ("sync", "gpsimd", "vector", "scalar", "tensor")
TAPS = False
DBG = False
DEBUG_MEM = False


class Sem:
    def __init__(self, stack, nc, name):
        self.h = stack.enter_context(nc.semaphore(name))
        self.v = 0

    def inc(self, n):
        self.v += n
        return self.v


class B:

    def __init__(self):
        self.th = {e: [] for e in ENGINES}
        self.last_wait = {e: {} for e in ENGINES}

    def E(self, eng, fn):
        self.th[eng].append(fn)

    def wait(self, eng, sem, val):
        if val is None or val <= 0:
            return
        k = id(sem)
        if self.last_wait[eng].get(k, -1) >= val:
            return
        self.last_wait[eng][k] = val
        self.E(eng, lambda e, s=sem.h, v=val: e.wait_ge(s, v))

    def waitp(self, eng, pair):
        if pair is not None:
            self.wait(eng, pair[0], pair[1])

    def dma(self, eng, sem, out, in_):
        tgt = sem.inc(16)
        self.E(eng, lambda e, o=out, i=in_, s=sem.h: e.dma_start(out=o, in_=i).then_inc(s, 16))
        return tgt

    def mm(self, out, lhsT, rhs, start, stop, sem=None):
        tgt = sem.inc(1) if sem else None
        sh = sem.h if sem else None

        def f(e, o=out, l=lhsT, r=rhs, st=start, sp=stop, s=sh):
            i = e.matmul(o, l, r, start=st, stop=sp)
            if s is not None:
                i.then_inc(s, 1)
        self.E("tensor", f)
        return tgt

    def op(self, eng, sem, meth, *args, **kw):
        tgt = sem.inc(1) if sem else None
        sh = sem.h if sem else None

        def f(e, m=meth, a=args, k=kw, s=sh):
            i = getattr(e, m)(*a, **k)
            if s is not None:
                i.then_inc(s, 1)
        self.E(eng, f)
        return tgt


def build_program():
    nc = bass.Bass("TRN2", target_bir_lowering=False)
    b = B()
    stack = ExitStack()
    DT = {}

    def din(name, shape, dtype):
        DT[name] = nc.dram_tensor(name, list(shape), dtype, kind="ExternalInput").ap()
        return DT[name]

    def dscr(name, shape, dtype):
        kind = "ExternalOutput" if TAPS else "Internal"
        DT[name] = nc.dram_tensor(name, list(shape), dtype, kind=kind).ap()
        return DT[name]

    XT = din("XT", (128, HC, TLOC), F32)
    W_IN = din("W_IN", (16, 128, HC * 128), F32)
    W_OUT1 = din("W_OUT1", (HC, 128, 56 * 128), F32)
    W_AGU = din("W_AGU", (2 * FC, 128, HC * 128), F32)
    W_AD = din("W_AD", (HC, 128, FC * 128), F32)
    QGU = din("QGU", (HC, 2, 128, II), I32)
    SGU = din("SGU", (2, 32, II), F32)
    QZGU = din("QZGU", (2, 32, II // 8), I32)
    QD = din("QD", (IC, 128, H), I32)
    SD = din("SD", (86, H), F32)
    QZD = din("QZD", (86, H // 8), I32)
    LPD = din("LPD", (128, 8), F32)
    DFD = din("DFD", (1, 8), F32)
    SEL32N = din("SEL32N", (128, 32 * 32), BF16)
    SEL86N = din("SEL86N", (86, 128, 86), BF16)
    BC32 = din("BC32", (32, 32, 128), BF16)
    BC86 = din("BC86", (86, 86, 128), BF16)
    ONESC = din("ONESC", (128, 1), BF16)
    ONESF = din("ONESF", (128, 1), F32)
    ONESR = din("ONESR", (1, 128), F32)
    SHIFTS = din("SHIFTS", (128, 1), I32)

    OUT = nc.dram_tensor("OUT", [128, HC, TLOC], F32, kind="ExternalOutput").ap()

    XN_SP = dscr("XN_SP", (128, HC * TLOC), BF16)
    AUT_SP = dscr("AUT_SP", (128, 56 * TLOC), BF16)
    HA_SP = dscr("HA_SP", (128, FC * TLOC), BF16)
    HG_SP = dscr("HG_SP", (128, 2, IC, 512), BF16)
    PA = dscr("PA", (128, HC, TLOC), F32)
    PB = dscr("PB", (128, HC, TLOC), F32)
    WDQ_GU = dscr("WDQ_GU", (HC, 2, 128, II), BF16)
    WDQ_D = dscr("WDQ_D", (IC, 128, H), BF16)
    SZP_GU = dscr("SZP_GU", (32, 2 * II), BF16)
    SZP_D = dscr("SZP_D", (86, H), BF16)

    sem_names = ["mm", "ev", "sc", "gp",
                 "ld_c", "ld_x", "ld_win", "ld_w1", "ld_agu", "ld_ad",
                 "ld_q", "ld_s", "ld_z", "ld_bc", "ld_sel",
                 "ld_xn", "ld_aut", "ld_ha", "ld_hg", "ld_w5", "ld_w6",
                 "ld_pax", "sp_xn", "sp_aut", "sp_ha", "sp_hg",
                 "sp_pa", "sp_pb", "wq_g", "wq_u", "wq_d", "sp_z", "ld_szp",
                 "out"]
    sm = {n: Sem(stack, nc, n) for n in sem_names}

    ps = stack.enter_context(nc.psum_tensor("ps", [128, 4096], F32)).ap()

    def bank(i):
        return ps[:, 512 * i:512 * (i + 1)]

    def bank2(i):
        return ps[:, 512 * i:512 * (i + 2)]

    bank_free = [None] * 8

    gc_ = nc.alloc_sbuf_tensor("g_consts", [128, 64], F32).ap()
    onesc_f = gc_[:, 0:1]
    lp_sb = gc_[:, 8:16]
    dfb = gc_[:, 16:24]
    onesc_b = nc.alloc_sbuf_tensor("onesc_b", [128, 1], BF16).ap()
    onesr = nc.alloc_sbuf_tensor("onesr", [1, 128], F32).ap()
    shifts = nc.alloc_sbuf_tensor("shifts", [128, 1], I32).ap()
    dfrow = nc.alloc_sbuf_tensor("dfrow", [1, 8], F32).ap()
    xsum_gu = nc.alloc_sbuf_tensor("xsum_gu", [32, TLOC], BF16).ap()
    xsum_d = nc.alloc_sbuf_tensor("xsum_d", [86, 512], BF16).ap()

    b.dma("sync", sm["ld_c"], gc_[:, 0:1], ONESF)
    b.dma("sync", sm["ld_c"], lp_sb, LPD)
    b.dma("sync", sm["ld_c"], dfrow, DFD)
    b.dma("sync", sm["ld_c"], onesr, ONESR)
    b.dma("sync", sm["ld_c"], shifts, SHIFTS)
    c_all = sm["ld_c"].v
    t_onescb = b.dma("sync", sm["ld_c"], onesc_b, ONESC)

    b.wait("scalar", sm["ld_c"], c_all)
    t_dfe = b.op("scalar", sm["sc"], "activation", dfrow, dfrow, AF.Exp, scale=-1.0)
    b.wait("tensor", sm["sc"], t_dfe)
    b.wait("tensor", sm["ld_c"], c_all)
    t0 = b.mm(ps[:, 0:8], onesr, dfrow, True, True, sm["mm"])
    b.wait("vector", sm["mm"], t0)
    t_dfb = b.op("vector", sm["ev"], "tensor_copy", dfb, ps[:, 0:8])
    bank_free[0] = (sm["ev"], t_dfb)

    xn_s = ExitStack()
    xn = xn_s.enter_context(nc.sbuf_tensor("xn1", [128, HC * 1024], BF16)).ap()
    s0 = ExitStack()
    xb = s0.enter_context(nc.sbuf_tensor("xb", [128, 2048], F32)).ap()
    xsq = s0.enter_context(nc.sbuf_tensor("xsq", [128, 2048], F32)).ap()
    srow = s0.enter_context(nc.sbuf_tensor("srow", [1, 4096], F32)).ap()
    sel32 = s0.enter_context(nc.sbuf_tensor("sel32", [128, 1024], BF16)).ap()
    t_sel32 = b.dma("sync", sm["ld_c"], sel32, SEL32N)

    xb_free = [None, None]
    t_stat = [None] * HC
    for c in range(HC):
        sl = c % 2
        b.waitp("sync", xb_free[sl])
        t_x = b.dma("sync", sm["ld_x"], xb[:, 1024 * sl:1024 * (sl + 1)], XT[:, c, :])
        b.wait("scalar", sm["ld_x"], t_x)
        if c >= 2:
            b.wait("scalar", sm["mm"], t_stat[c - 2])
        t_sq = b.op("scalar", sm["sc"], "activation",
                    xsq[:, 1024 * sl:1024 * (sl + 1)], xb[:, 1024 * sl:1024 * (sl + 1)],
                    AF.Square)
        b.wait("tensor", sm["sc"], t_sq)
        b.wait("tensor", sm["ld_c"], c_all)
        if c == 0:
            b.waitp("tensor", bank_free[0])
        for h2 in range(2):
            b.mm(ps[0:1, 512 * h2:512 * (h2 + 1)], onesc_f,
                 xb[:, 1024 * sl + 512 * h2:1024 * sl + 512 * (h2 + 1)],
                 c == 0, c == HC - 1, None)
            b.mm(ps[0:1, 1024 + 512 * h2:1024 + 512 * (h2 + 1)], onesc_f,
                 xsq[:, 1024 * sl + 512 * h2:1024 * sl + 512 * (h2 + 1)],
                 c == 0, c == HC - 1, sm["mm"] if h2 == 1 else None)
        t_stat[c] = sm["mm"].v
        xb_free[sl] = (sm["mm"], t_stat[c])

    mean, var = srow[0:1, 0:1024], srow[0:1, 1024:2048]
    arow, brow = srow[0:1, 2048:3072], srow[0:1, 3072:4096]
    b.wait("vector", sm["mm"], t_stat[HC - 1])
    b.op("vector", None, "tensor_scalar", mean, ps[0:1, 0:1024], 1.0 / H, None, ALU.mult)
    b.op("vector", None, "tensor_scalar", var, ps[0:1, 1024:2048], 1.0 / H, None, ALU.mult)
    b.op("vector", None, "tensor_tensor", brow, mean, mean, ALU.mult)
    t_v = b.op("vector", sm["ev"], "tensor_tensor", var, var, brow, ALU.subtract)
    for i in range(4):
        bank_free[i] = (sm["ev"], t_v)
    t_eps = b.op("vector", sm["ev"], "tensor_scalar", var, var, EPS, None, ALU.add)
    b.wait("scalar", sm["ev"], t_eps)
    t_sd = b.op("scalar", sm["sc"], "activation", var, var, AF.Sqrt)
    b.wait("vector", sm["sc"], t_sd)
    b.op("vector", None, "reciprocal", arow, var)
    t_ab = b.op("vector", sm["ev"], "tensor_scalar", brow, mean, -1.0, None, ALU.mult)
    b.wait("tensor", sm["ev"], t_ab)
    for h2 in range(2):
        b.mm(ps[:, 2048 + 512 * h2:2560 + 512 * h2], onesr,
             arow[0:1, 512 * h2:512 * (h2 + 1)], True, True, None)
        b.mm(ps[:, 3072 + 512 * h2:3584 + 512 * h2], onesr,
             brow[0:1, 512 * h2:512 * (h2 + 1)], True, True,
             sm["mm"] if h2 == 1 else None)
    t_bcast = sm["mm"].v
    ps_a, ps_b = ps[:, 2048:3072], ps[:, 3072:4096]

    tmpf = [xsq[:, 0:1024], xsq[:, 1024:2048]]
    b.wait("vector", sm["mm"], t_bcast)
    xb_free = [(sm["mm"], t_stat[HC - 1])] * 2
    t_xn_last = 0
    for c in range(HC):
        sl = c % 2
        b.waitp("sync", xb_free[sl])
        t_x = b.dma("sync", sm["ld_x"], xb[:, 1024 * sl:1024 * (sl + 1)], XT[:, c, :])
        b.wait("vector", sm["ld_x"], t_x)
        xnc = xn[:, 1024 * c:1024 * (c + 1)]
        b.op("vector", None, "tensor_tensor", tmpf[sl],
             xb[:, 1024 * sl:1024 * (sl + 1)], ps_b, ALU.add)
        t_xn = b.op("vector", sm["ev"], "tensor_tensor", xnc, tmpf[sl], ps_a, ALU.mult)
        xb_free[sl] = (sm["ev"], t_xn)
        t_xn_last = t_xn
        b.wait("scalar", sm["ev"], t_xn)
        b.dma("scalar", sm["sp_xn"], XN_SP[:, 1024 * c:1024 * (c + 1)], xnc)
        b.wait("tensor", sm["ev"], t_xn)
        b.wait("tensor", sm["ld_c"], t_sel32)
        if c == 0:
            b.waitp("tensor", bank_free[0])
        for h2 in range(2):
            b.mm(ps[0:32, 512 * h2:512 * (h2 + 1)], sel32[:, 32 * c:32 * (c + 1)],
                 xnc[:, 512 * h2:512 * (h2 + 1)], c == 0, c == HC - 1,
                 sm["mm"] if (c == HC - 1 and h2 == 1) else None)
    t_xsum = sm["mm"].v
    b.wait("vector", sm["mm"], t_xsum)
    t_xs = b.op("vector", sm["ev"], "tensor_copy", xsum_gu, ps[0:32, 0:1024])
    bank_free[0] = bank_free[1] = (sm["ev"], t_xs)
    for i in range(4, 8):
        bank_free[i] = (sm["ev"], t_xn_last)
    sp_xn_all = sm["sp_xn"].v
    s0.close()

    def phase_gate(eng, *dma_sems):
        for s_ in ("mm", "ev", "sc", "gp"):
            b.wait(eng, sm[s_], sm[s_].v)
        for s_ in dma_sems:
            b.wait(eng, sm[s_], sm[s_].v)

    class Deq:
        def __init__(self, dq, which):
            self.which = which
            mat = {"g": 0, "u": 1, "d": 2}[which]
            self.nch = HC if which != "d" else IC
            self.ncols = II if which != "d" else H
            self.nzg = 32 if which != "d" else 86
            self.wq = sm["wq_" + which]
            nzg, ncols = self.nzg, self.ncols

            self.sres = dq.enter_context(
                nc.sbuf_tensor(f"sres_{which}", [nzg, ncols], BF16)).ap()
            self.t_s = b.dma("gpsimd", sm["ld_s"], self.sres,
                             SGU[mat] if which != "d" else SD)
            zs_scope = ExitStack()
            qz = zs_scope.enter_context(nc.sbuf_tensor(f"qz_{which}", [nzg, ncols // 8], I32)).ap()
            zt = zs_scope.enter_context(nc.sbuf_tensor(f"zt_{which}", [nzg, ncols // 8], I32)).ap()
            zf = zs_scope.enter_context(nc.sbuf_tensor(f"zf_{which}", [nzg, ncols // 8], F32)).ap()
            szp = zs_scope.enter_context(nc.sbuf_tensor(f"szp_{which}", [nzg, ncols], BF16)).ap()
            t_qz = b.dma("sync", sm["ld_z"], qz, QZGU[mat] if which != "d" else QZD)
            b.wait("vector", sm["ld_z"], t_qz)
            b.wait("vector", sm["ld_s"], self.t_s)
            t_z = 0
            for jj in range(8):
                t_sh = b.op("vector", sm["ev"], "tensor_scalar", zt, qz, 4 * jj, 15,
                            ALU.logical_shift_right, ALU.bitwise_and)
                b.wait("gpsimd", sm["ev"], t_sh)
                t_zf = b.op("gpsimd", sm["gp"], "tensor_copy", zf, zt)
                b.wait("vector", sm["gp"], t_zf)
                dst = szp.rearrange("g (rh rl) -> g rh rl", rl=128)[:, :, 16 * jj:16 * jj + 16]
                ssrc = self.sres.rearrange("g (rh rl) -> g rh rl", rl=128)[:, :, 16 * jj:16 * jj + 16]
                zsrc = zf.rearrange("g (rh rl) -> g rh rl", rl=16)
                t_z = b.op("vector", sm["ev"], "scalar_tensor_tensor", dst, zsrc,
                           -7.0, ssrc, ALU.add, ALU.mult)
            b.wait("scalar", sm["ev"], t_z)
            self.t_szp_sp = b.dma("scalar", sm["sp_z"],
                                  SZP_GU[:, mat * II:(mat + 1) * II] if which != "d" else SZP_D, szp)
            zs_scope.close()

            self.qb = dq.enter_context(nc.sbuf_tensor(f"qb_{which}", [128, 2048], I32)).ap()
            self.w4i = dq.enter_context(nc.sbuf_tensor(f"w4i_{which}", [128, 1024], I32)).ap()
            self.w4f = dq.enter_context(nc.sbuf_tensor(f"w4f_{which}", [128, 1024], F32)).ap()
            self.wo = dq.enter_context(nc.sbuf_tensor(f"wo_{which}", [128, 2048], BF16)).ap()
            self.bcb = dq.enter_context(
                nc.sbuf_tensor(f"bcb_{which}", [nzg, 256], BF16)).ap()
            self.strip = lambda c: (QGU[c, mat] if which != "d" else QD[c])
            self.wdst = lambda c: (WDQ_GU[c, mat] if which != "d" else WDQ_D[c])
            self.bc_src = (BC32 if which != "d" else BC86)
            self.gr = [(c, g0, min(1024, ncols - g0))
                       for c in range(self.nch) for g0 in range(0, ncols, 1024)]
            self.i = 0
            self.qb_free = [None, None]
            self.wo_free = [None, None]
            self.bc_free = [0, 0]
            self.sb_free = [bank_free[4], bank_free[6]]
            self.gp_prev = 0
            self.mu_prev = 0
            self.t_bc = 0
            self.cur_c = -1

        def done(self):
            return self.i >= len(self.gr)

        def step(self, n):
            for _ in range(n):
                if self.done():
                    return
                c, g0, gcols = self.gr[self.i]
                sl = self.i % 2
                if c != self.cur_c:
                    self.cur_c = c
                    bsl = c % 2
                    self.bsl = bsl
                    b.wait("gpsimd", sm["mm"], self.bc_free[bsl])
                    self.t_bc = b.dma("gpsimd", sm["ld_bc"],
                                      self.bcb[0:self.nzg, 128 * bsl:128 * (bsl + 1)],
                                      self.bc_src[c])
                b.waitp("sync", self.qb_free[sl])
                b.wait("sync", sm["sp_z"], self.t_szp_sp)
                t_q = b.dma("sync", sm["ld_q"], self.qb[:, 1024 * sl:1024 * sl + gcols],
                            self.strip(c)[:, g0:g0 + gcols])
                psl = 4 + 2 * sl
                b.waitp("tensor", self.sb_free[sl])
                b.wait("tensor", sm["ld_bc"], self.t_bc)
                b.wait("tensor", sm["ld_s"], self.t_s)
                t_sb = None
                for q0 in range(0, gcols, 512):
                    qc = min(512, gcols - q0)
                    bcs = self.bcb[0:self.nzg, 128 * self.bsl:128 * (self.bsl + 1)]
                    t_sb = b.mm(ps[:, 512 * psl + q0:512 * psl + q0 + qc], bcs,
                                self.sres[0:self.nzg, g0 + q0:g0 + q0 + qc],
                                True, True,
                                sm["mm"] if q0 + 512 >= gcols else None)
                b.wait("vector", sm["ld_q"], t_q)
                b.wait("vector", sm["gp"], self.gp_prev)
                t_sh = b.op("vector", sm["ev"], "tensor_scalar",
                            self.w4i[:, 0:gcols], self.qb[:, 1024 * sl:1024 * sl + gcols],
                            shifts, 15, ALU.logical_shift_right, ALU.bitwise_and)
                self.qb_free[sl] = (sm["ev"], t_sh)
                b.wait("gpsimd", sm["ev"], max(t_sh, self.mu_prev))
                self.gp_prev = b.op("gpsimd", sm["gp"], "tensor_copy",
                                    self.w4f[:, 0:gcols], self.w4i[:, 0:gcols])
                b.wait("vector", sm["gp"], self.gp_prev)
                b.wait("vector", sm["mm"], t_sb)
                b.waitp("vector", self.wo_free[sl])
                self.mu_prev = b.op("vector", sm["ev"], "scalar_tensor_tensor",
                                    self.wo[:, 1024 * sl:1024 * sl + gcols],
                                    self.w4f[:, 0:gcols], -8.0,
                                    ps[:, 512 * psl:512 * psl + gcols],
                                    ALU.add, ALU.mult)
                self.sb_free[sl] = (sm["ev"], self.mu_prev)
                self.bc_free[self.bsl] = t_sb
                b.wait("scalar", sm["ev"], self.mu_prev)
                twq = b.dma("scalar", self.wq, self.wdst(c)[:, g0:g0 + gcols],
                            self.wo[:, 1024 * sl:1024 * sl + gcols])
                self.wo_free[sl] = (self.wq, twq)
                self.i += 1

        def finish(self):
            self.step(len(self.gr))
            bank_free[4] = bank_free[5] = self.sb_free[0]
            bank_free[6] = bank_free[7] = self.sb_free[1]

    s1 = ExitStack()
    win = s1.enter_context(nc.sbuf_tensor("win", [128, 2 * HC * 128], BF16)).ap()
    au_f = s1.enter_context(nc.sbuf_tensor("au_f", [128, 1024], F32)).ap()
    delta = s1.enter_context(nc.sbuf_tensor("delta", [128, 1024], F32)).ap()
    keep = s1.enter_context(nc.sbuf_tensor("keep", [128, 1024], F32)).ap()
    qpow = s1.enter_context(nc.sbuf_tensor("qpow", [128, 1024], F32)).ap()
    tser = s1.enter_context(nc.sbuf_tensor("tser", [128, 1024], F32)).ap()
    tbf = s1.enter_context(nc.sbuf_tensor("tbf", [128, 2048], BF16)).ap()
    phase_gate("sync", "sp_xn")
    phase_gate("vector")
    phase_gate("gpsimd")
    dq_g = ExitStack()
    deq_g = Deq(dq_g, "g")

    win_free = [None, None]
    tbf_free = [None, None]
    ser_free = None
    for s in range(SC):
        for half in range(2):
            o = 2 * s + half
            sl = o % 2
            b.waitp("gpsimd", win_free[sl])
            t_w = b.dma("gpsimd", sm["ld_win"],
                        win[:, sl * HC * 128:(sl + 1) * HC * 128], W_IN[o])
            b.wait("tensor", sm["ld_win"], t_w)
            for h2 in range(2):
                psl = 2 * half + h2
                b.waitp("tensor", bank_free[psl])
                for k in range(HC):
                    b.mm(bank(psl),
                         win[:, sl * HC * 128 + 128 * k:sl * HC * 128 + 128 * (k + 1)],
                         xn[:, 1024 * k + 512 * h2:1024 * k + 512 * (h2 + 1)],
                         k == 0, k == HC - 1, sm["mm"] if k == HC - 1 else None)
            win_free[sl] = (sm["mm"], sm["mm"].v)
        t_gate_grp = sm["mm"].v
        t_up_grp = t_gate_grp - 2
        b.wait("scalar", sm["mm"], t_up_grp)
        b.waitp("scalar", ser_free)
        t_auf = b.op("scalar", sm["sc"], "activation", au_f, ps[:, 0:1024], AF.Copy)
        b.wait("vector", sm["sc"], t_auf)
        b.waitp("vector", tbf_free[0])
        t_aubf = b.op("vector", sm["ev"], "tensor_copy", tbf[:, 0:1024], au_f)
        bank_free[0] = bank_free[1] = (sm["ev"], t_aubf)
        b.wait("scalar", sm["ev"], t_aubf)
        tsp = b.dma("scalar", sm["sp_aut"], AUT_SP[:, 1024 * s:1024 * (s + 1)],
                    tbf[:, 0:1024])
        tbf_free[0] = (sm["sp_aut"], tsp)
        b.wait("vector", sm["mm"], t_gate_grp)
        t_dl = b.op("vector", sm["ev"], "tensor_scalar", delta, ps[:, 1024:2048],
                    lp_sb[:, s:s + 1], None, ALU.subtract)
        bank_free[2] = bank_free[3] = (sm["ev"], t_dl)
        b.wait("scalar", sm["ev"], t_dl)
        t_abs = b.op("scalar", sm["sc"], "activation", tser, delta, AF.Abs)
        b.wait("vector", sm["sc"], t_abs)
        b.op("vector", None, "tensor_scalar", keep, tser, THD, None, ALU.is_le)
        t_t = None
        for m in range(1, G + 1):
            src_q = delta if m == 1 else qpow
            b.op("vector", None, "tensor_tensor", tser, src_q, au_f, ALU.mult)
            tsl = m % 2
            b.waitp("vector", tbf_free[tsl])
            dst = tbf[:, 1024 * tsl:1024 * (tsl + 1)]
            if m > 4:
                b.op("vector", None, "tensor_scalar", tser, tser,
                     dfb[:, m - 1:m], None, ALU.mult)
                t_t = b.op("vector", sm["ev"], "tensor_tensor", dst, tser, keep, ALU.mult)
            else:
                t_t = b.op("vector", sm["ev"], "tensor_scalar", dst, tser,
                           dfb[:, m - 1:m], None, ALU.mult)
            b.wait("scalar", sm["ev"], t_t)
            tsp = b.dma("scalar", sm["sp_aut"],
                        AUT_SP[:, 1024 * (8 * m + s):1024 * (8 * m + s + 1)], dst)
            tbf_free[tsl] = (sm["sp_aut"], tsp)
            if m < G:
                b.op("vector", None, "tensor_tensor", qpow, src_q, delta, ALU.mult)
        ser_free = (sm["ev"], t_t)
        deq_g.step(43)
    deq_g.finish()
    sp_aut_all = sm["sp_aut"].v
    dq_g.close()
    s1.close()
    xn_s.close()

    s2 = ExitStack()
    aut = s2.enter_context(nc.sbuf_tensor("aut", [128, 56 * 1024], BF16)).ap()
    w1b = s2.enter_context(nc.sbuf_tensor("w1b", [128, 2 * 56 * 128], BF16)).ap()
    pab = s2.enter_context(nc.sbuf_tensor("pab", [128, 2 * 1024], F32)).ap()
    phase_gate("gpsimd", "wq_g", "sp_xn")
    phase_gate("sync", "wq_g", "sp_xn", "sp_z", "sp_aut")
    phase_gate("vector", "wq_g")
    dq_d = ExitStack()
    deq_d = Deq(dq_d, "d")

    b.wait("sync", sm["sp_aut"], sp_aut_all)
    phase_gate("sync", "wq_g", "sp_xn", "sp_z")
    t_aut = b.dma("sync", sm["ld_aut"], aut, AUT_SP)
    w1_free = [None, None]
    pab_free = [None, None]
    for o in range(HC):
        sl = o % 2
        b.waitp("gpsimd", w1_free[sl])
        t_w = b.dma("gpsimd", sm["ld_w1"],
                    w1b[:, sl * 56 * 128:(sl + 1) * 56 * 128], W_OUT1[o])
        b.wait("tensor", sm["ld_w1"], t_w)
        b.wait("tensor", sm["ld_aut"], t_aut)
        for h2 in range(2):
            psl = 2 * (o % 2) + h2
            b.waitp("tensor", bank_free[psl])
            for k in range(56):
                b.mm(bank(psl),
                     w1b[:, sl * 56 * 128 + 128 * k:sl * 56 * 128 + 128 * (k + 1)],
                     aut[:, 1024 * k + 512 * h2:1024 * k + 512 * (h2 + 1)],
                     k == 0, k == 55, sm["mm"] if k == 55 else None)
        t_grp = sm["mm"].v
        w1_free[sl] = (sm["mm"], t_grp)
        b.wait("scalar", sm["mm"], t_grp)
        b.waitp("scalar", pab_free[sl])
        t_pa = b.op("scalar", sm["sc"], "activation",
                    pab[:, 1024 * sl:1024 * (sl + 1)], bank2(2 * (o % 2)), AF.Copy)
        bank_free[2 * (o % 2)] = bank_free[2 * (o % 2) + 1] = (sm["sc"], t_pa)
        tsp = b.dma("scalar", sm["sp_pa"], PA[:, o, :], pab[:, 1024 * sl:1024 * (sl + 1)])
        pab_free[sl] = (sm["sp_pa"], tsp)
        deq_d.step(11)
    deq_d.finish()
    if DBG:
        DBG_AUT = dscr("DBG_AUT", (128, 56 * 1024), BF16)
        DBG_W1B = dscr("DBG_W1B", (128, 2 * 56 * 128), BF16)
        b.wait("sync", sm["mm"], sm["mm"].v)
        b.dma("sync", sm["ld_c"], DBG_AUT, aut)
        b.dma("sync", sm["ld_c"], DBG_W1B, w1b)
        b.wait("sync", sm["ld_c"], sm["ld_c"].v)
    dq_d.close()
    s2.close()

    s3 = ExitStack()
    xn3 = s3.enter_context(nc.sbuf_tensor("xn3", [128, HC * 1024], BF16)).ap()
    wgb = s3.enter_context(nc.sbuf_tensor("wgb", [128, 2 * HC * 128], BF16)).ap()
    sil = s3.enter_context(nc.sbuf_tensor("sil", [128, 1024], F32)).ap()
    hab = s3.enter_context(nc.sbuf_tensor("hab", [128, 2048], BF16)).ap()
    phase_gate("gpsimd", "wq_d", "sp_pa")
    phase_gate("sync", "wq_d", "sp_pa", "sp_z")
    phase_gate("vector", "wq_d")
    dq_u = ExitStack()
    deq_u = Deq(dq_u, "u")

    b.wait("sync", sm["sp_xn"], sp_xn_all)
    phase_gate("sync", "wq_d", "sp_pa", "sp_z")
    t_xn3 = b.dma("sync", sm["ld_xn"], xn3, XN_SP)
    wg_free = [None, None]
    hab_free = [None, None]
    sil_free = None
    for cf in range(FC):
        for half in range(2):
            o = 2 * cf + half
            sl = o % 2
            b.waitp("gpsimd", wg_free[sl])
            t_w = b.dma("gpsimd", sm["ld_agu"],
                        wgb[:, sl * HC * 128:(sl + 1) * HC * 128], W_AGU[o])
            b.wait("tensor", sm["ld_agu"], t_w)
            b.wait("tensor", sm["ld_xn"], t_xn3)
            for h2 in range(2):
                psl = 2 * half + h2
                b.waitp("tensor", bank_free[psl])
                for k in range(HC):
                    b.mm(bank(psl),
                         wgb[:, sl * HC * 128 + 128 * k:sl * HC * 128 + 128 * (k + 1)],
                         xn3[:, 1024 * k + 512 * h2:1024 * k + 512 * (h2 + 1)],
                         k == 0, k == HC - 1, sm["mm"] if k == HC - 1 else None)
            wg_free[sl] = (sm["mm"], sm["mm"].v)
        t_up = sm["mm"].v
        t_gate = t_up - 2
        b.wait("scalar", sm["mm"], t_gate)
        b.waitp("scalar", sil_free)
        t_sil = b.op("scalar", sm["sc"], "activation", sil, ps[:, 0:1024], AF.Silu)
        bank_free[0] = bank_free[1] = (sm["sc"], t_sil)
        b.wait("vector", sm["sc"], t_sil)
        b.wait("vector", sm["mm"], t_up)
        hsl = cf % 2
        b.waitp("vector", hab_free[hsl])
        t_h = b.op("vector", sm["ev"], "tensor_tensor",
                   hab[:, 1024 * hsl:1024 * (hsl + 1)], sil, ps[:, 1024:2048], ALU.mult)
        sil_free = (sm["ev"], t_h)
        bank_free[2] = bank_free[3] = (sm["ev"], t_h)
        b.wait("scalar", sm["ev"], t_h)
        tsp = b.dma("scalar", sm["sp_ha"], HA_SP[:, 1024 * cf:1024 * (cf + 1)],
                    hab[:, 1024 * hsl:1024 * (hsl + 1)])
        hab_free[hsl] = (sm["sp_ha"], tsp)
        deq_u.step(5)
    deq_u.finish()
    sp_ha_all = sm["sp_ha"].v
    dq_u.close()
    s3.close()

    s4 = ExitStack()
    ha = s4.enter_context(nc.sbuf_tensor("ha", [128, FC * 1024], BF16)).ap()
    wdb = s4.enter_context(nc.sbuf_tensor("wdb", [128, 2 * FC * 128], BF16)).ap()
    pbb = s4.enter_context(nc.sbuf_tensor("pbb", [128, 2048], F32)).ap()
    b.wait("sync", sm["sp_ha"], sp_ha_all)
    phase_gate("sync", "wq_u", "sp_z")
    phase_gate("gpsimd", "wq_u")
    t_ha = b.dma("sync", sm["ld_ha"], ha, HA_SP)
    wd_free = [None, None]
    pbb_free = [None, None]
    for o in range(HC):
        sl = o % 2
        b.waitp("gpsimd", wd_free[sl])
        t_w = 0
        for spi in range(2):
            t_w = b.dma("gpsimd", sm["ld_ad"],
                        wdb[:, sl * FC * 128 + spi * 39 * 128:
                            sl * FC * 128 + (spi + 1) * 39 * 128],
                        W_AD[o][:, spi * 39 * 128:(spi + 1) * 39 * 128])
        b.wait("tensor", sm["ld_ad"], t_w)
        b.wait("tensor", sm["ld_ha"], t_ha)
        for h2 in range(2):
            psl = 2 * (o % 2) + h2
            b.waitp("tensor", bank_free[psl])
            for k in range(FC):
                b.mm(bank(psl),
                     wdb[:, sl * FC * 128 + 128 * k:sl * FC * 128 + 128 * (k + 1)],
                     ha[:, 1024 * k + 512 * h2:1024 * k + 512 * (h2 + 1)],
                     k == 0, k == FC - 1, sm["mm"] if k == FC - 1 else None)
        t_grp = sm["mm"].v
        wd_free[sl] = (sm["mm"], t_grp)
        b.wait("scalar", sm["mm"], t_grp)
        b.waitp("scalar", pbb_free[sl])
        t_pb = b.op("scalar", sm["sc"], "activation",
                    pbb[:, 1024 * sl:1024 * (sl + 1)], bank2(2 * (o % 2)), AF.Copy)
        bank_free[2 * (o % 2)] = bank_free[2 * (o % 2) + 1] = (sm["sc"], t_pb)
        tsp = b.dma("scalar", sm["sp_pb"], PB[:, o, :], pbb[:, 1024 * sl:1024 * (sl + 1)])
        pbb_free[sl] = (sm["sp_pb"], tsp)
    s4.close()

    s5 = ExitStack()
    xn5 = s5.enter_context(nc.sbuf_tensor("xn5", [128, HC * 1024], BF16)).ap()
    w5b = s5.enter_context(nc.sbuf_tensor("w5b", [128, 4 * HC * 256], BF16)).ap()
    szp_gu5 = s5.enter_context(nc.sbuf_tensor("szp_gu5", [32, 2 * II], BF16)).ap()
    sil5 = s5.enter_context(nc.sbuf_tensor("sil5", [128, 1024], F32)).ap()
    hgb = s5.enter_context(nc.sbuf_tensor("hgb", [128, 2048], BF16)).ap()
    b.wait("sync", sm["sp_xn"], sp_xn_all)
    phase_gate("sync", "sp_pb")
    t_xn5 = b.dma("sync", sm["ld_xn"], xn5, XN_SP)
    b.wait("sync", sm["sp_z"], sm["sp_z"].v)
    t_szp = b.dma("sync", sm["ld_szp"], szp_gu5, SZP_GU)
    wq_g_all, wq_u_all = sm["wq_g"].v, sm["wq_u"].v

    quad_free = [None, None]
    sil5_free = None
    hgb_free = [None, None]
    for q in range(IC // 2):
        sl = q % 2
        b.waitp("sync", quad_free[sl])
        b.wait("sync", sm["wq_g"], wq_g_all)
        b.wait("sync", sm["wq_u"], wq_u_all)
        t_w = [0, 0]
        for mi in range(2):
            src = WDQ_GU[:, mi, :, 256 * q:256 * (q + 1)].transpose([1, 0, 2])
            dst = w5b[:, (2 * sl + mi) * HC * 256:(2 * sl + mi + 1) * HC * 256] \
                .rearrange("p (kc m) -> p kc m", m=256)
            t_w[mi] = b.dma("sync", sm["ld_w5"], dst, src)
        grp_t = {}
        for sub in range(2):
            cf = 2 * q + sub
            for mi in range(2):
                base = (2 * sl + mi) * HC * 256 + sub * 128
                b.wait("tensor", sm["ld_w5"], t_w[mi])
                b.wait("tensor", sm["ld_xn"], t_xn5)
                b.wait("tensor", sm["ld_szp"], t_szp)
                for h2 in range(2):
                    psl = 4 * (q % 2) + 2 * mi + h2
                    b.waitp("tensor", bank_free[psl])
                    for k in range(HC):
                        b.mm(bank(psl), w5b[:, base + 256 * k:base + 256 * k + 128],
                             xn5[:, 1024 * k + 512 * h2:1024 * k + 512 * (h2 + 1)],
                             k == 0, False, None)
                    b.mm(bank(psl), szp_gu5[:, mi * II + 128 * cf:mi * II + 128 * (cf + 1)],
                         xsum_gu[0:32, 512 * h2:512 * (h2 + 1)],
                         False, True, sm["mm"])
                grp_t[mi] = sm["mm"].v
            pb0 = 4 * (q % 2)
            b.wait("scalar", sm["mm"], grp_t[0])
            b.waitp("scalar", sil5_free)
            t_sil = b.op("scalar", sm["sc"], "activation", sil5,
                         ps[:, 512 * pb0:512 * pb0 + 1024], AF.Silu)
            bank_free[pb0] = bank_free[pb0 + 1] = (sm["sc"], t_sil)
            b.wait("vector", sm["sc"], t_sil)
            b.wait("vector", sm["mm"], grp_t[1])
            hsl = cf % 2
            b.waitp("vector", hgb_free[hsl])
            t_h = b.op("vector", sm["ev"], "tensor_tensor",
                       hgb[:, 1024 * hsl:1024 * (hsl + 1)], sil5,
                       ps[:, 512 * (pb0 + 2):512 * (pb0 + 2) + 1024], ALU.mult)
            sil5_free = (sm["ev"], t_h)
            bank_free[pb0 + 2] = bank_free[pb0 + 3] = (sm["ev"], t_h)
            tsp = 0
            b.wait("scalar", sm["ev"], t_h)
            for h2 in range(2):
                tsp = b.dma("scalar", sm["sp_hg"], HG_SP[:, h2, cf, :],
                            hgb[:, 1024 * hsl + 512 * h2:1024 * hsl + 512 * (h2 + 1)])
            hgb_free[hsl] = (sm["sp_hg"], tsp)
        quad_free[sl] = (sm["mm"], sm["mm"].v)
    sp_hg_all = sm["sp_hg"].v
    s5.close()

    s6 = ExitStack()
    hg = s6.enter_context(nc.sbuf_tensor("hg", [128, IC * 512], BF16)).ap()
    w6b = s6.enter_context(nc.sbuf_tensor("w6b", [128, 2 * IC * 128], BF16)).ap()
    szp_d6 = s6.enter_context(nc.sbuf_tensor("szp_d6", [86, H], BF16)).ap()
    sel86 = s6.enter_context(nc.sbuf_tensor("sel86", [128, 2 * 86], BF16)).ap()
    resb = s6.enter_context(nc.sbuf_tensor("resb", [128, 6 * 512], F32)).ap()
    tadd = s6.enter_context(nc.sbuf_tensor("tadd", [128, 512], F32)).ap()
    obuf = s6.enter_context(nc.sbuf_tensor("obuf", [128, 2 * 512], F32)).ap()
    b.wait("sync", sm["sp_z"], sm["sp_z"].v)
    t_szd = b.dma("sync", sm["ld_szp"], szp_d6, SZP_D)
    wq_d_all = sm["wq_d"].v

    hg_ld = None
    phase_gate("sync")
    phase_gate("gpsimd")
    for half in range(2):
        b.wait("sync", sm["sp_hg"], sp_hg_all)
        if hg_ld is not None:
            b.wait("sync", sm["mm"], hg_ld)
        t_hg = b.dma("sync", sm["ld_hg"], hg,
                     HG_SP[:, half, :, :].rearrange("p c t -> p (c t)")
                     if False else HG_SP[:, half])
        selb_free = None
        b.waitp("tensor", bank_free[4])
        for c in range(IC):
            ssl = c % 2
            b.waitp("gpsimd", selb_free)
            t_sel = b.dma("gpsimd", sm["ld_sel"], sel86[:, 86 * ssl:86 * (ssl + 1)],
                          SEL86N[c])
            b.wait("tensor", sm["ld_sel"], t_sel)
            b.wait("tensor", sm["ld_hg"], t_hg)
            t_pm = b.mm(ps[0:86, 2048:2560], sel86[:, 86 * ssl:86 * (ssl + 1)],
                        hg[:, 512 * c:512 * (c + 1)], c == 0, c == IC - 1, sm["mm"])
            selb_free = (sm["mm"], t_pm)
        t_xsd = sm["mm"].v
        b.wait("vector", sm["mm"], t_xsd)
        t_xsde = b.op("vector", sm["ev"], "tensor_copy", xsum_d, ps[0:86, 2048:2560])
        bank_free[4] = (sm["ev"], t_xsde)

        w6_free = [None, None]
        res_free = [None, None]
        obuf_free = [None, None]
        for o in range(HC):
            sl = o % 2
            b.waitp("sync", res_free[sl])
            b.wait("sync", sm["sp_pa"], sm["sp_pa"].v)
            b.wait("sync", sm["sp_pb"], sm["sp_pb"].v)
            tr = b.dma("sync", sm["ld_pax"], resb[:, 512 * sl:512 * (sl + 1)],
                       PA[:, o, 512 * half:512 * (half + 1)])
            tr = b.dma("sync", sm["ld_pax"], resb[:, 1024 + 512 * sl:1536 + 512 * sl],
                       PB[:, o, 512 * half:512 * (half + 1)])
            tr = b.dma("sync", sm["ld_pax"], resb[:, 2048 + 512 * sl:2560 + 512 * sl],
                       XT[:, o, 512 * half:512 * (half + 1)])
            b.waitp("sync", w6_free[sl])
            b.wait("sync", sm["wq_d"], wq_d_all)
            src = WDQ_D[:, :, 128 * o:128 * (o + 1)].transpose([1, 0, 2])
            dst = w6b[:, sl * IC * 128:(sl + 1) * IC * 128] \
                .rearrange("p (c m) -> p c m", m=128)
            t_w6a = b.dma("sync", sm["ld_w6"],
                          dst[:, 0:43, :], src[:, 0:43, :])
            t_w6 = b.dma("sync", sm["ld_w6"],
                         dst[:, 43:86, :], src[:, 43:86, :])
            psl = o % 2
            b.wait("tensor", sm["ld_w6"], t_w6)
            b.wait("tensor", sm["ld_hg"], t_hg)
            b.wait("tensor", sm["ld_szp"], t_szd)
            b.wait("tensor", sm["ev"], t_xsde)
            b.waitp("tensor", bank_free[psl])
            for k in range(IC):
                b.mm(bank(psl),
                     w6b[:, sl * IC * 128 + 128 * k:sl * IC * 128 + 128 * (k + 1)],
                     hg[:, 512 * k:512 * (k + 1)], k == 0, False, None)
            b.mm(bank(psl), szp_d6[0:86, 128 * o:128 * (o + 1)], xsum_d[0:86, :],
                 False, True, sm["mm"])
            t_grp = sm["mm"].v
            w6_free[sl] = (sm["mm"], t_grp)
            b.wait("vector", sm["mm"], t_grp)
            b.wait("vector", sm["ld_pax"], tr)
            b.op("vector", None, "tensor_tensor", tadd, bank(psl),
                 resb[:, 512 * sl:512 * (sl + 1)], ALU.add)
            b.op("vector", None, "tensor_tensor", tadd, tadd,
                 resb[:, 1024 + 512 * sl:1536 + 512 * sl], ALU.add)
            b.waitp("vector", obuf_free[sl])
            t_o = b.op("vector", sm["ev"], "tensor_tensor", obuf[:, 512 * sl:512 * (sl + 1)],
                       tadd, resb[:, 2048 + 512 * sl:2560 + 512 * sl], ALU.add)
            bank_free[psl] = (sm["ev"], t_o)
            res_free[sl] = (sm["ev"], t_o)
            b.wait("scalar", sm["ev"], t_o)
            t_out = b.dma("scalar", sm["out"], OUT[:, o, 512 * half:512 * (half + 1)],
                          obuf[:, 512 * sl:512 * (sl + 1)])
            obuf_free[sl] = (sm["out"], t_out)
        hg_ld = sm["mm"].v
    s6.close()

    b.wait("scalar", sm["out"], sm["out"].v)
    b.wait("sync", sm["out"], sm["out"].v)

    if DEBUG_MEM:
        print("sbuf bytes remaining:", nc.sbuf_bytes_remaining)

    blk = stack.enter_context(nc.Block())
    for eng in ENGINES:
        fns = b.th[eng]

        def runner(e, fns=fns):
            for f in fns:
                f(e)
        getattr(blk, eng)(runner)
    stack.close()
    return nc, DT



def _tile_stationary(WT):
    K, O = WT.shape
    kc, oc = K // 128, O // 128
    t = WT.reshape(kc, 128, oc, 128).transpose(2, 1, 0, 3)
    return np.ascontiguousarray(t.reshape(oc, 128, kc * 128))


def prep_inputs(inp):
    a = {}
    f32 = np.float32

    up_w = np.asarray(inp["L0_attn_up_w"], f32)
    gate_w = np.asarray(inp["L0_attn_gate_w"], f32)
    lao = np.asarray(inp["L0_attn_local_approx_output"], f32)
    fuse = np.asarray(inp["L0_attn_fuse_weight"], f32)
    ffg = np.asarray(inp["L0_attn_ffn_gate_w"], f32)
    ffu = np.asarray(inp["L0_attn_ffn_up_w"], f32)
    ffd = np.asarray(inp["L0_attn_ffn_down_w"], f32)

    WT_all = np.concatenate([up_w.T, gate_w.T], axis=1)[DEV_H]
    cols = np.concatenate([np.concatenate(
        [np.arange(128 * s, 128 * (s + 1)), np.arange(S + 128 * s, S + 128 * (s + 1))])
        for s in range(SC)])
    a["W_IN"] = _tile_stationary(np.ascontiguousarray(WT_all[:, cols]))

    V = np.concatenate([lao.T, fuse.transpose(2, 1, 0).reshape(G * S, H)], axis=0)
    a["W_OUT1"] = _tile_stationary(np.ascontiguousarray(V[:, DEV_H]))

    AG = np.concatenate([ffg.T, ffu.T], axis=1)[DEV_H]
    cols = np.concatenate([np.concatenate(
        [np.arange(128 * c, 128 * (c + 1)), np.arange(FF + 128 * c, FF + 128 * (c + 1))])
        for c in range(FC)])
    a["W_AGU"] = _tile_stationary(np.ascontiguousarray(AG[:, cols]))

    a["W_AD"] = _tile_stationary(np.ascontiguousarray(ffd.T[:, DEV_H]))

    qg = np.asarray(inp["L0_ffn_gate_qweight"], np.int32)
    qu = np.asarray(inp["L0_ffn_up_qweight"], np.int32)
    DEV_I_ = _dev_order(II)
    rows_idx = 16 * np.arange(HC)[:, None] + (np.arange(128) % 16)[None, :]
    qgu = np.stack([qg[:, DEV_I_], qu[:, DEV_I_]], axis=0)
    a["QGU"] = np.ascontiguousarray(
        qgu[:, rows_idx, :].transpose(1, 0, 2, 3)
    ).reshape(HC, 2, 128, II)
    sg = np.asarray(inp["L0_ffn_gate_scales"], f32)[:, DEV_I_]
    su = np.asarray(inp["L0_ffn_up_scales"], f32)[:, DEV_I_]
    a["SGU"] = np.ascontiguousarray(np.stack([sg, su], axis=0))
    a["QZGU"] = np.ascontiguousarray(np.stack(
        [np.asarray(inp["L0_ffn_gate_qzeros"], np.int32),
         np.asarray(inp["L0_ffn_up_qzeros"], np.int32)], axis=0))

    qd = np.asarray(inp["L0_ffn_down_qweight"], np.int32)
    rows_idx_d = 16 * np.arange(IC)[:, None] + (np.arange(128) % 16)[None, :]
    a["QD"] = np.ascontiguousarray(qd[:, DEV_H][rows_idx_d, :])
    a["SD"] = np.ascontiguousarray(np.asarray(inp["L0_ffn_down_scales"], f32)[:, DEV_H])
    a["QZD"] = np.ascontiguousarray(np.asarray(inp["L0_ffn_down_qzeros"], np.int32))

    lp = np.asarray(inp["L0_attn_local_point"], f32)
    a["LPD"] = np.ascontiguousarray(lp.reshape(8, 128).T)
    df = np.zeros((1, 8), f32)
    df[0, :G] = np.asarray(inp["L0_attn_discount_factor"], f32)
    a["DFD"] = df

    bf = ml_dtypes.bfloat16
    s32 = np.zeros((128, 32, 32), bf)
    for c in range(32):
        s32[:, c, c] = -1.0
    a["SEL32N"] = np.ascontiguousarray(s32.reshape(128, 1024))
    s86 = np.zeros((86, 128, 86), bf)
    for c in range(86):
        s86[c, :, c] = -1.0
    a["SEL86N"] = s86
    bc32 = np.zeros((32, 32, 128), bf)
    for g in range(32):
        bc32[g, g, :] = 1.0
    a["BC32"] = bc32
    bc86 = np.zeros((86, 86, 128), bf)
    for g in range(86):
        bc86[g, g, :] = 1.0
    a["BC86"] = bc86
    a["ONESC"] = np.ones((128, 1), bf)
    a["ONESF"] = np.ones((128, 1), f32)
    a["ONESR"] = np.ones((1, 128), f32)
    a["SHIFTS"] = (4 * (np.arange(128, dtype=np.int32) // 16)).reshape(128, 1)

    x = np.asarray(inp["L0_attn_hidden_states"], f32).reshape(8192, H)
    xdev = np.ascontiguousarray(x.T)[DEV_H]
    xts = []
    for c in range(NCORES):
        blkx = xdev[:, TLOC * c:TLOC * (c + 1)].reshape(HC, 128, TLOC)
        xts.append(np.ascontiguousarray(blkx.transpose(1, 0, 2)))
    return a, xts


_CACHE = {}
LAST_RES = None


def kernel(**inputs):
    global LAST_RES
    arrs, xts = prep_inputs(inputs)
    if "nc" not in _CACHE:
        _CACHE["nc"], _CACHE["dt"] = build_program()
    nc = _CACHE["nc"]
    in_maps = [dict(arrs, XT=xts[c]) for c in range(NCORES)]
    res = run_bass_kernel_spmd(nc, in_maps, core_ids=list(range(NCORES)))
    LAST_RES = res
    odev = np.empty((H, 8192), np.float32)
    o32 = odev.reshape(HC, 128, 8192)
    for c in range(NCORES):
        blko = np.asarray(res.results[c]["OUT"])
        o32[:, PERM, TLOC * c:TLOC * (c + 1)] = blko.transpose(1, 0, 2)
    return np.ascontiguousarray(odev.T).reshape(2, 4096, H)
